# version 02
# baseline (speedup 1.0000x reference)
import contextlib
import numpy as np
import ml_dtypes
import concourse.bass as bass
import concourse.mybir as mybir
from concourse.bass_utils import run_bass_kernel_spmd
from concourse.alu_op_type import AluOpType as ALU

F32 = mybir.dt.float32
BF16 = mybir.dt.bfloat16
I32 = mybir.dt.int32
AF = mybir.ActivationFunctionType
AX = mybir.AxisListType

T = 2048
D = 4096
DC = 32
H = 32
HD = 128
LAT = 512
LC = 4
INTER = 11008
EPS = 1e-5
ISCALE = float(1.0 / np.sqrt(HD))
PI = float(np.pi)
TWO_PI = float(2.0 * np.pi)


def _nhat_blocks():
    out = []
    for c_hi in range(11):
        w = 128 if c_hi < 10 else 96
        for k in range(8):
            base = (c_hi * 1024 + k * 128) if c_hi < 10 else (10240 + k * 96)
            out.append((base, w, c_hi, k))
    return out

NHB = _nhat_blocks()

def _nhat_perm():
    p = np.empty(INTER, dtype=np.int64)
    for base, w, c_hi, k in NHB:
        c = c_hi * 128 + np.arange(w)
        p[base:base + w] = 8 * c + k
    return p

def _khat_perm():
    p = np.empty(D, dtype=np.int64)
    for k2 in range(8):
        c2 = np.arange(512)
        p[k2 * 512 + c2] = 8 * c2 + k2
    return p

NHAT = _nhat_perm()
KHAT = _khat_perm()


class Ctx:
    def __init__(self, detect_races=True):
        self.nc = bass.Bass("TRN2", target_bir_lowering=False,
                            detect_race_conditions=detect_races)
        self.stack = contextlib.ExitStack()
        self.block = None
        self.n = {}
        self.sems = {}
        self.b = {}

    def start(self):
        for s in ["ld", "lg", "ls", "pe", "dv", "ac", "gp"]:
            self.sems[s] = self.stack.enter_context(self.nc.semaphore(s))
            self.n[s] = 0
        self.block = self.stack.enter_context(self.nc.Block())

    def sbuf(self, name, shape, dtype):
        t = self.stack.enter_context(self.nc.sbuf_tensor(name, shape, dtype))
        self.b[name] = t
        return t

    def psum(self, name, shape, dtype=F32):
        t = self.stack.enter_context(self.nc.psum_tensor(name, shape, dtype))
        self.b[name] = t
        return t

    def sec(self, eng):
        return getattr(self.block, eng)

    def i(self, sem, by=1):
        self.n[sem] += by
        return self.sems[sem]

    def s(self, sem):
        return self.sems[sem]

    def barrier(self):
        counts = dict(self.n)

        for eng in ["sync", "gpsimd", "vector", "scalar", "tensor"]:
            @self.sec(eng)
            def _(e):
                for sm in ["ld", "lg", "ls", "pe", "dv", "ac", "gp"]:
                    e.wait_ge(self.s(sm), counts[sm])

    def fin(self):
        nld, nlg, nls = self.n["ld"], self.n["lg"], self.n["ls"]

        @self.sec("sync")
        def _(q):
            q.wait_ge(self.s("ld"), nld)
            q.wait_ge(self.s("lg"), nlg)
            q.wait_ge(self.s("ls"), nls)

    def close(self):
        self.stack.close()


def mm_tok(t, pdst, lhsT, rhs, start, stop, poff=0, coff=0, cols=T):
    last = None
    for jb in range(0, cols, 512):
        last = t.matmul(pdst[:, poff + jb:poff + jb + 512], lhsT,
                        rhs[:, coff + jb:coff + jb + 512], start=start,
                        stop=stop)
    return last


def wslice_ap(w_ap, row_stride, col0, ncols, nrows):
    nch = nrows // 128
    return bass.AP(w_ap.tensor, col0,
                   [[row_stride, 128], [row_stride * 128, nch], [1, ncols]])


def alloc_common(c, kind):
    c.psum("pA", [128, T], F32)
    c.psum("pB", [128, T], F32)
    c.sbuf("lnb", [128, DC * T], BF16)
    c.sbuf("BIG0", [128, 12288], BF16)
    c.sbuf("BIG1", [128, 12288], BF16)
    c.sbuf("WA", [128, 4096], BF16)
    c.sbuf("WB", [128, 4096], BF16)
    c.b["rwb"] = c.b["WB"].bitcast(F32)[0:1, :]
    c.b["rwa"] = c.b["WA"].bitcast(F32)[0:1, 0:T]
    c.b["rb1"] = c.b["BIG1"].bitcast(F32)[0:1, 0:T]
    c.b["r8"] = c.b["WB"].bitcast(F32)[0:8, :]
    c.sbuf("ones_bf", [1, 128], BF16)
    c.sbuf("ones_cb", [128, 1], BF16)
    c.sbuf("ones_f", [1, 128], F32)
    c.sbuf("eps_t", [1, 1], F32)
    c.sbuf("bias_t", [128, 1], F32)
    c.sbuf("sc1", [4, 8], F32)
    if kind in ("attn0", "fused"):
        c.sbuf("rpm", [128, 128], BF16)
        c.sbuf("cs", [128, 2 * T], BF16)
    if kind == "fused":
        c.b["FF0"] = c.b["cs"][:, 0:4096]
    if kind in ("ffn0", "fused"):
        if kind == "ffn0":
            c.sbuf("FF0", [128, 4096], BF16)
        c.sbuf("GS", [32, T], BF16)
        c.sbuf("QZ", [32, 256], I32)
        c.sbuf("SZ", [32, 512], BF16)
        c.b["HS"] = c.b["lnb"][0:86, 45232:45744]
        c.b["SD"] = c.b["WA"][0:86, 0:4096]
        c.b["ZSD"] = c.b["WB"][0:86, 0:4096]
        c.b["QZD"] = c.b["FF0"].bitcast(I32)[0:86, 0:512]
        c.b["MB"] = c.b["FF0"][:, 256:384]
        c.b["MB2"] = c.b["lnb"][:, 45056:45232]
    if kind in ("ffn1", "fused"):
        c.sbuf("IND", [8, 128], BF16)
        c.sbuf("BB", [128, 32], F32)
        if kind == "ffn1":
            c.sbuf("identf", [128, 128], F32)
        else:
            c.b["identf"] = c.b["BIG0"].bitcast(F32)[:, 3392:3520]

    @c.sec("gpsimd")
    def _(g):
        def step(inst):
            inst.then_inc(c.i("gp"), 1)
            g.wait_ge(c.s("gp"), c.n["gp"])
        g.memset(c.b["ones_bf"][:1, :], 1.0)
        g.memset(c.b["ones_cb"][:, 0:1], 1.0)
        g.memset(c.b["ones_f"][:1, :], 1.0)
        g.memset(c.b["eps_t"][:1, :], EPS)
        last = g.memset(c.b["sc1"][:, :], 0.0)
        if kind in ("attn0", "fused"):
            rpm, scr = c.b["rpm"], c.b["BIG0"]
            step(g.memset(rpm[:, :], 1.0))
            step(g.memset(rpm[64:128, :], -1.0))
            step(g.affine_select(scr[:, 0:128], rpm[:, :],
                                 pattern=[[-1, 128]],
                                 compare_op=ALU.is_equal, fill=0.0,
                                 base=-64, channel_multiplier=1))
            step(g.affine_select(rpm[:, :], rpm[:, :], pattern=[[-1, 128]],
                                 compare_op=ALU.is_equal, fill=0.0,
                                 base=64, channel_multiplier=1))
            last = g.tensor_tensor(rpm[:, :], rpm[:, :], scr[:, 0:128],
                                   op=ALU.add)
        if kind in ("ffn1", "fused"):
            if kind == "ffn1":
                step(g.memset(c.b["identf"][:, :], 1.0))
                step(g.affine_select(c.b["identf"][:, :],
                                     c.b["identf"][:, :],
                                     pattern=[[-1, 128]],
                                     compare_op=ALU.is_equal, fill=0.0,
                                     base=0, channel_multiplier=1))
            step(g.memset(c.b["IND"][:8, :], 2.0))
            step(g.affine_select(c.b["IND"][:8, :], c.b["IND"][:8, :],
                                 pattern=[[1, 128]], compare_op=ALU.is_ge,
                                 fill=0.0, base=0, channel_multiplier=-16))
            last = g.affine_select(c.b["IND"][:8, :], c.b["IND"][:8, :],
                                   pattern=[[-1, 128]],
                                   compare_op=ALU.is_ge, fill=0.0,
                                   base=15, channel_multiplier=16)
        last.then_inc(c.i("gp"), 1)


def emit_rope_tables(c):
    r0 = c.b["rwb"]
    r1 = c.b["rb1"]
    posr = c.b["rwa"]
    pA = c.b["pA"]
    tmpf = c.b["BIG0"].bitcast(F32)
    gp0, dv0, ac0, pe0 = c.n["gp"], c.n["dv"], c.n["ac"], c.n["pe"]

    @c.sec("gpsimd")
    def _(g):
        g.wait_ge(c.s("gp"), gp0)
        g.iota(r0[0:1, 0:128], pattern=[[1, 128]], base=0,
               channel_multiplier=0, allow_small_or_imprecise_dtypes=True)
        g.iota(posr, pattern=[[1, T]], base=0, channel_multiplier=0,
               allow_small_or_imprecise_dtypes=True).then_inc(c.i("gp"), 1)

    @c.sec("vector")
    def _(v):
        v.wait_ge(c.s("gp"), gp0 + 1)
        v.tensor_scalar(r1[0:1, 0:128], r0[0:1, 0:128], -64.0, None,
                        op0=ALU.add).then_inc(c.i("dv"), 1)
        v.wait_ge(c.s("dv"), c.n["dv"])
        v.tensor_scalar(r1[0:1, 0:128], r1[0:1, 0:128], 0.0, None,
                        op0=ALU.is_ge).then_inc(c.i("dv"), 1)
        v.wait_ge(c.s("dv"), c.n["dv"])
        v.scalar_tensor_tensor(r0[0:1, 0:128], r1[0:1, 0:128], -64.0,
                               r0[0:1, 0:128], op0=ALU.mult,
                               op1=ALU.add).then_inc(c.i("dv"), 1)

    @c.sec("scalar")
    def _(s):
        s.wait_ge(c.s("dv"), dv0 + 3)
        s.activation(r0[0:1, 0:128], r0[0:1, 0:128], AF.Exp, bias=0.0,
                     scale=float(-np.log(10000.0) / 64.0)
                     ).then_inc(c.i("ac"), 1)

    @c.sec("tensor")
    def _(t):
        t.wait_ge(c.s("ac"), ac0 + 1)
        mm_tok(t, pA, r0[0:1, 0:128], posr, True,
               True).then_inc(c.i("pe"), 1)

    for which in ("cos", "sin"):
        shift = float(np.pi / 2) if which == "cos" else 0.0
        off = 0 if which == "cos" else T
        dvb = c.n["dv"]
        acb = c.n["ac"]

        @c.sec("vector")
        def _(v, shift=shift, acb=acb):
            v.wait_ge(c.s("pe"), pe0 + 1)
            if which == "sin":
                v.wait_ge(c.s("ac"), acb)
            v.tensor_scalar(tmpf[:, 0:T], pA[:, :], shift,
                            float(1.0 / TWO_PI), op0=ALU.add, op1=ALU.mult)
            v.tensor_scalar(tmpf[:, 0:T], tmpf[:, 0:T], float(2.0 ** 23),
                            float(2.0 ** 23), op0=ALU.add, op1=ALU.subtract)
            v.scalar_tensor_tensor(tmpf[:, 0:T], tmpf[:, 0:T], -TWO_PI,
                                   pA[:, :], op0=ALU.mult, op1=ALU.add)
            if shift:
                v.tensor_scalar(tmpf[:, 0:T], tmpf[:, 0:T], shift, None,
                                op0=ALU.add)
            v.tensor_scalar(tmpf[:, 0:T], tmpf[:, 0:T], -PI, PI,
                            op0=ALU.max, op1=ALU.min).then_inc(c.i("dv"), 1)

        @c.sec("scalar")
        def _(s, off=off, dvb=dvb):
            s.wait_ge(c.s("dv"), dvb + 1)
            s.activation(c.b["cs"][:, off:off + T], tmpf[:, 0:T], AF.Sin,
                         bias=0.0, scale=1.0).then_inc(c.i("ac"), 1)


def emit_ln(c, x_dram):
    pA, pB = c.b["pA"], c.b["pB"]
    b0f = c.b["BIG0"].bitcast(F32)
    xs = [b0f[:, 0:T], b0f[:, T:2 * T]]
    tmpf = b0f[:, 2 * T:3 * T]
    xb = c.b["BIG1"][:, 0:T]
    xq = c.b["BIG1"][:, T:2 * T]
    lnb = c.b["lnb"]
    xv = x_dram.rearrange("(cc p) t -> cc p t", p=128)
    ld0, dv0, ac0, pe0 = c.n["ld"], c.n["dv"], c.n["ac"], c.n["pe"]

    @c.sec("sync")
    def _(q):
        q.wait_ge(c.s("dv"), dv0)
        q.wait_ge(c.s("ac"), ac0)
        for cc in range(DC):
            if cc >= 2:
                q.wait_ge(c.s("dv"), dv0 + cc - 1)
                q.wait_ge(c.s("ac"), ac0 + cc - 1)
            q.dma_start(out=xs[cc % 2][:, :],
                        in_=xv[cc, :, :]).then_inc(c.i("ld", 16), 16)

    @c.sec("vector")
    def _(v):
        for cc in range(DC):
            v.wait_ge(c.s("ld"), ld0 + 16 * (cc + 1))
            if cc >= 1:
                v.wait_ge(c.s("pe"), pe0 + cc)
            v.tensor_copy(xb, xs[cc % 2][:, :]).then_inc(c.i("dv"), 1)

    @c.sec("scalar")
    def _(s):
        for cc in range(DC):
            s.wait_ge(c.s("ld"), ld0 + 16 * (cc + 1))
            if cc >= 1:
                s.wait_ge(c.s("pe"), pe0 + cc)
            s.square(xq, xs[cc % 2][:, :]).then_inc(c.i("ac"), 1)

    @c.sec("tensor")
    def _(t):
        for cc in range(DC):
            t.wait_ge(c.s("dv"), dv0 + cc + 1)
            t.wait_ge(c.s("ac"), ac0 + cc + 1)
            st, sp = (cc == 0), (cc == DC - 1)
            mm_tok(t, pA[0:1, :], c.b["ones_cb"][:, 0:1], xb, st, sp)
            last = None
            for jb in range(0, T, 512):
                last = t.matmul(pB[0:1, jb:jb + 512],
                                c.b["ones_cb"][:, 0:1],
                                xq[:, jb:jb + 512], start=st, stop=sp)
            last.then_inc(c.i("pe"), 1)

    dv1, ac1, pe1 = c.n["dv"], c.n["ac"], c.n["pe"]

    rm, rr, ra = c.b["rb1"], c.b["rwb"], c.b["rwa"]

    @c.sec("vector")
    def _(v):
        v.wait_ge(c.s("pe"), pe1)
        v.tensor_copy(rm, pA[0:1, :])
        v.tensor_copy(rr[0:1, 0:T], pB[0:1, :])
        v.tensor_scalar(rm, rm, float(1.0 / D), None, op0=ALU.mult)
        v.tensor_scalar(rr[0:1, 0:T], rr[0:1, 0:T], float(1.0 / D),
                        None, op0=ALU.mult)
        v.tensor_tensor(ra, rm, rm, op=ALU.mult)
        v.tensor_tensor(rr[0:1, 0:T], rr[0:1, 0:T], ra,
                        op=ALU.subtract).then_inc(c.i("dv"), 1)

    @c.sec("scalar")
    def _(s):
        s.wait_ge(c.s("dv"), dv1 + 1)
        s.activation(rr[0:1, 0:T], rr[0:1, 0:T], AF.Ln,
                     bias=c.b["eps_t"][:1, 0:1], scale=1.0)
        s.activation(rr[0:1, 0:T], rr[0:1, 0:T], AF.Exp, bias=0.0,
                     scale=-0.5).then_inc(c.i("ac"), 1)

    @c.sec("vector")
    def _(v):
        v.wait_ge(c.s("ac"), ac1 + 1)
        v.tensor_tensor(ra, rm, rr[0:1, 0:T], op=ALU.mult)
        v.tensor_scalar(ra, ra, -1.0, None, op0=ALU.mult).then_inc(
            c.i("dv"), 1)

    dv2 = c.n["dv"]

    @c.sec("tensor")
    def _(t):
        t.wait_ge(c.s("dv"), dv2)
        mm_tok(t, pA, c.b["ones_f"][:1, :], rr[0:1, 0:T], True, True)
        mm_tok(t, pB, c.b["ones_f"][:1, :], ra, True,
               True).then_inc(c.i("pe"), 1)

    pe2, ld1 = c.n["pe"], c.n["ld"]

    @c.sec("sync")
    def _(q):
        q.wait_ge(c.s("dv"), dv2)
        q.wait_ge(c.s("ac"), ac1 + 1)
        for cc in range(DC):
            if cc >= 2:
                q.wait_ge(c.s("dv"), dv2 + 2 * (cc - 1))
            q.dma_start(out=xs[cc % 2][:, :],
                        in_=xv[cc, :, :]).then_inc(c.i("ld", 16), 16)

    @c.sec("vector")
    def _(v):
        v.wait_ge(c.s("pe"), pe2)
        for cc in range(DC):
            v.wait_ge(c.s("ld"), ld1 + 16 * (cc + 1))
            v.tensor_tensor(tmpf[:, :], xs[cc % 2][:, :], pA[:, :],
                            op=ALU.mult).then_inc(c.i("dv"), 1)
            v.tensor_tensor(lnb[:, cc * T:(cc + 1) * T], tmpf[:, :],
                            pB[:, :], op=ALU.add).then_inc(c.i("dv"), 1)


def emit_norm_max(c, src_bf, sqr, dst_slot):
    pB = c.b["pB"]
    ac0, pe0, dv0 = c.n["ac"], c.n["pe"], c.n["dv"]

    @c.sec("scalar")
    def _(s):
        s.wait_ge(c.s("pe"), pe0)
        s.wait_ge(c.s("dv"), dv0)
        s.square(sqr, src_bf).then_inc(c.i("ac"), 1)

    @c.sec("tensor")
    def _(t):
        t.wait_ge(c.s("ac"), ac0 + 1)
        t.wait_ge(c.s("dv"), dv0)
        last = None
        for jb in range(0, T, 512):
            last = t.matmul(pB[0:1, jb:jb + 512], c.b["ones_cb"][:, 0:1],
                            sqr[:, jb:jb + 512], start=True, stop=True)
        last.then_inc(c.i("pe"), 1)

    @c.sec("vector")
    def _(v):
        v.wait_ge(c.s("pe"), pe0 + 1)
        v.reduce_max(dst_slot, pB[0:1, :],
                     axis=AX.X).then_inc(c.i("dv"), 1)
        v.wait_ge(c.s("dv"), c.n["dv"])


def emit_bias_from_norms(c, qslot, kslot):
    sc = c.b["sc1"]
    dv0, ac0 = c.n["dv"], c.n["ac"]

    @c.sec("vector")
    def _(v):
        v.wait_ge(c.s("dv"), dv0)
        v.tensor_tensor(sc[0:1, 2:3], qslot, kslot,
                        op=ALU.mult).then_inc(c.i("dv"), 1)

    @c.sec("scalar")
    def _(s):
        s.wait_ge(c.s("dv"), dv0 + 1)
        s.activation(sc[0:1, 2:3], sc[0:1, 2:3], AF.Ln, bias=0.0, scale=1.0)
        s.activation(sc[0:1, 2:3], sc[0:1, 2:3], AF.Exp, bias=0.0,
                     scale=0.5).then_inc(c.i("ac"), 1)

    @c.sec("vector")
    def _(v):
        v.wait_ge(c.s("ac"), ac0 + 1)
        v.tensor_scalar(sc[0:1, 2:3], sc[0:1, 2:3], -ISCALE, None,
                        op0=ALU.mult).then_inc(c.i("dv"), 1)

    dv1, pe1 = c.n["dv"], c.n["pe"]

    @c.sec("tensor")
    def _(t):
        t.wait_ge(c.s("dv"), dv1)
        t.matmul(c.b["pB"][:, 3:4], c.b["ones_f"][:1, :], sc[0:1, 2:3],
                 start=True, stop=True).then_inc(c.i("pe"), 1)

    @c.sec("vector")
    def _(v):
        v.wait_ge(c.s("pe"), pe1 + 1)
        v.tensor_copy(c.b["bias_t"][:, 0:1],
                      c.b["pB"][:, 3:4]).then_inc(c.i("dv"), 1)


def emit_attn_core(c, q_sb, k_sb, vg, vg_col, at_head_rows, causal):
    pA, pB = c.b["pA"], c.b["pB"]
    WA = c.b["WA"]
    eT = [WA[:, 0:1024], WA[:, 1024:2048]]
    atb = WA[:, 2048:3072]
    rb_b = WA[:, 3072:4096]
    r2 = c.b["rwb"]
    bias_dv = c.n["dv"]

    for ih in range(2):
        i0 = ih * 1024
        pS = [pA[:, 0:1024], pA[:, 1024:2048]]
        jts = [jt for jt in range(16)
               if (not causal) or (128 * jt < i0 + 1024)]

        def blocks_for(jt):
            if not causal:
                return [0, 1]
            return [ib for ib in range(2) if i0 + (ib + 1) * 512 > 128 * jt]

        first_jt, last_jt = {}, {}
        for jt in jts:
            for ib in blocks_for(jt):
                first_jt.setdefault(ib, jt)
                last_jt[ib] = jt

        def diag(jt):
            return causal and (128 * jt + 127 >= i0)

        pe0, ac0, gp0 = c.n["pe"], c.n["ac"], c.n["gp"]
        nj = len(jts)

        def avsums(t, jt, idx):
            last = None
            for ib in blocks_for(jt):
                st, sp = (first_jt[ib] == jt), (last_jt[ib] == jt)
                t.matmul(pB[:, ib * 512:(ib + 1) * 512],
                         vg[:, vg_col(jt):vg_col(jt) + 128],
                         eT[idx % 2][:, ib * 512:(ib + 1) * 512],
                         start=st, stop=sp)
                last = t.matmul(
                    pB[0:1, 1024 + ib * 512:1024 + (ib + 1) * 512],
                    c.b["ones_cb"][:, 0:1],
                    eT[idx % 2][:, ib * 512:(ib + 1) * 512],
                    start=st, stop=sp)
            return last

        dv_entry = c.n["dv"]

        @c.sec("tensor")
        def _(t):
            t.wait_ge(c.s("dv"), dv_entry)
            for idx, jt in enumerate(jts):
                if idx >= 2:
                    t.wait_ge(c.s("ac"), ac0 + idx - 1)
                last = None
                for ib in blocks_for(jt):
                    last = t.matmul(
                        pS[idx % 2][:, ib * 512:(ib + 1) * 512],
                        k_sb[:, jt * 128:(jt + 1) * 128],
                        q_sb[:, i0 + ib * 512:i0 + (ib + 1) * 512],
                        start=True, stop=True)
                last.then_inc(c.i("pe"), 1)
                if idx >= 1:
                    pj = jts[idx - 1]
                    if diag(pj):
                        t.wait_ge(c.s("gp"), gp0 + idx)
                    else:
                        t.wait_ge(c.s("ac"), ac0 + idx)
                    avsums(t, pj, idx - 1)
            pj = jts[nj - 1]
            if diag(pj):
                t.wait_ge(c.s("gp"), gp0 + nj)
            else:
                t.wait_ge(c.s("ac"), ac0 + nj)
            avsums(t, pj, nj - 1).then_inc(c.i("pe"), 1)

        @c.sec("scalar")
        def _(s):
            s.wait_ge(c.s("dv"), bias_dv)
            for idx, jt in enumerate(jts):
                s.wait_ge(c.s("pe"), pe0 + idx + 1)
                last = None
                for ib in blocks_for(jt):
                    last = s.activation(
                        eT[idx % 2][:, ib * 512:(ib + 1) * 512],
                        pS[idx % 2][:, ib * 512:(ib + 1) * 512],
                        AF.Exp, bias=c.b["bias_t"][:, 0:1], scale=ISCALE)
                last.then_inc(c.i("ac"), 1)

        if causal:
            @c.sec("gpsimd")
            def _(g):
                for idx, jt in enumerate(jts):
                    g.wait_ge(c.s("ac"), ac0 + idx + 1)
                    if diag(jt):
                        last = None
                        for ib in blocks_for(jt):
                            last = g.affine_select(
                                eT[idx % 2][:, ib * 512:(ib + 1) * 512],
                                eT[idx % 2][:, ib * 512:(ib + 1) * 512],
                                pattern=[[1, 512]], compare_op=ALU.is_ge,
                                fill=0.0, base=i0 + ib * 512 - 128 * jt,
                                channel_multiplier=-1)
                        last.then_inc(c.i("gp"), 1)
                    else:
                        g.engine_nop().then_inc(c.i("gp"), 1)

        pe_core, dvb = c.n["pe"], c.n["dv"]

        @c.sec("vector")
        def _(v):
            v.wait_ge(c.s("pe"), pe_core)
            v.tensor_copy(r2[0:1, 0:1024], pB[0:1, 1024:2048])
            v.reciprocal(r2[0:1, 0:1024],
                         r2[0:1, 0:1024]).then_inc(c.i("dv"), 1)

        @c.sec("tensor")
        def _(t):
            t.wait_ge(c.s("dv"), dvb + 1)
            last = None
            for jb in range(2):
                last = t.matmul(pA[:, jb * 512:(jb + 1) * 512],
                                c.b["ones_f"][:1, :],
                                r2[0:1, jb * 512:(jb + 1) * 512],
                                start=True, stop=True)
            last.then_inc(c.i("pe"), 1)

        dvc = c.n["dv"]

        @c.sec("vector")
        def _(v):
            v.wait_ge(c.s("pe"), pe_core + 1)
            v.tensor_copy(rb_b, pA[:, 0:1024]).then_inc(c.i("dv"), 1)
            v.tensor_tensor(atb, pB[:, 0:1024], rb_b,
                            op=ALU.mult).then_inc(c.i("dv"), 1)

        @c.sec("sync")
        def _(q):
            q.wait_ge(c.s("dv"), dvc + 2)
            q.dma_start(out=at_head_rows[:, i0:i0 + 1024],
                        in_=atb).then_inc(c.i("ld", 16), 16)


def _l0_qk(c, w_d, rowstride, col0, nrows, lnsrc, nch, dst_sb, wb2,
           lat=None):
    pA = c.b["pA"]
    pre = c.b["WA"][:, 0:2048]
    cs = c.b["cs"]
    rpm = c.b["rpm"]
    dv0 = c.n["dv"]

    @c.sec("gpsimd")
    def _(g):
        g.wait_ge(c.s("dv"), dv0)
        g.dma_start(out=wb2[:, 0:(nrows // 128) * 128],
                    in_=wslice_ap(w_d, rowstride, col0, 128, nrows)
                    ).then_inc(c.i("lg", 16), 16)

    pe0, lg0 = c.n["pe"], c.n["lg"]

    @c.sec("tensor")
    def _(t):
        t.wait_ge(c.s("lg"), lg0)
        t.wait_ge(c.s("dv"), dv0)
        last = None
        for ch in range(nch):
            if lat is None:
                last = mm_tok(t, pA, wb2[:, ch * 128:(ch + 1) * 128],
                              lnsrc, ch == 0, ch == nch - 1, coff=ch * T)
            else:
                last = mm_tok(t, pA, wb2[:, ch * 128:(ch + 1) * 128],
                              lat[ch], ch == 0, ch == nch - 1)
        last.then_inc(c.i("pe"), 1)

    ac0 = c.n["ac"]

    @c.sec("scalar")
    def _(s):
        s.wait_ge(c.s("pe"), pe0 + 1)
        s.copy(pre, pA[:, :]).then_inc(c.i("ac"), 1)

    @c.sec("tensor")
    def _(t):
        t.wait_ge(c.s("ac"), ac0 + 1)
        mm_tok(t, pA, rpm[:, :], pre, True, True).then_inc(c.i("pe"), 1)

    @c.sec("vector")
    def _(v):
        v.wait_ge(c.s("pe"), pe0 + 2)
        v.tensor_tensor(dst_sb[:, :], pre, cs[:, 0:T], op=ALU.mult)
        v.tensor_tensor(pre, pA[:, :], cs[:, T:2 * T], op=ALU.mult)
        v.tensor_tensor(dst_sb[:, :], dst_sb[:, :], pre,
                        op=ALU.add).then_inc(c.i("dv"), 1)


ATTN_GROUPS = 8
DBG_T2 = False
DBG_LC = LC


def emit_attn_l0(c, wq_d, wc_d, wk_d, wv_d, at_scr):
    pA = c.b["pA"]
    lnb = c.b["lnb"]
    BIG0, BIG1, WB = c.b["BIG0"], c.b["BIG1"], c.b["WB"]
    lat = [BIG0[:, i * T:(i + 1) * T] for i in range(4)]
    q_sb = BIG0[:, 8192:10240]
    k_sb = BIG0[:, 10240:12288]
    vg = BIG1[:, 0:8192]
    wb2 = BIG1[:, 8192:12288]
    sqr = c.b["WA"][:, 0:2048]
    sc = c.b["sc1"]
    atv = at_scr.rearrange("(h p) t -> h p t", p=128)

    for lc2 in range(DBG_LC):
        wdst = WB if lc2 % 2 == 0 else wb2
        dv_pre = c.n["dv"]

        @c.sec("gpsimd")
        def _(g, wdst=wdst, lc2=lc2):
            g.wait_ge(c.s("dv"), dv_pre)
            if lc2 >= 2:
                g.wait_ge(c.s("pe"), c.n["pe"] - 1)
            g.dma_start(out=wdst[:, 0:4096],
                        in_=wslice_ap(wc_d, LAT, lc2 * 128, 128, D)
                        ).then_inc(c.i("lg", 16), 16)

        lgL = c.n["lg"]

        @c.sec("tensor")
        def _(t, wdst=wdst):
            t.wait_ge(c.s("lg"), lgL)
            t.wait_ge(c.s("dv"), dv_pre)
            last = None
            for ch in range(DC):
                last = mm_tok(t, pA, wdst[:, ch * 128:(ch + 1) * 128],
                              lnb, ch == 0, ch == DC - 1, coff=ch * T)
            last.then_inc(c.i("pe"), 1)

        peL = c.n["pe"]

        @c.sec("vector")
        def _(v, lc2=lc2):
            v.wait_ge(c.s("pe"), peL)
            v.tensor_copy(lat[lc2], pA[:, :]).then_inc(c.i("dv"), 1)

    for grp in range(ATTN_GROUPS):
        dvv0 = c.n["dv"]

        @c.sec("gpsimd")
        def _(g):
            g.wait_ge(c.s("dv"), dvv0)
            g.dma_start(out=WB[:, 0:2048],
                        in_=wslice_ap(wv_d, D, grp * 512, 512, LAT)
                        ).then_inc(c.i("lg", 16), 16)

        pev, lgv = c.n["pe"], c.n["lg"]

        @c.sec("tensor")
        def _(t):
            t.wait_ge(c.s("lg"), lgv)
            t.wait_ge(c.s("dv"), dvv0)
            for tt in range(16):
                if tt >= 4 and tt % 4 == 0:
                    t.wait_ge(c.s("dv"), dvv0 + tt // 4)
                last = None
                for lc2 in range(LC):
                    last = t.matmul(
                        pA[:, (tt % 4) * 512:(tt % 4) * 512 + 512],
                        lat[lc2][:, tt * 128:(tt + 1) * 128],
                        WB[:, lc2 * 512:(lc2 + 1) * 512],
                        start=(lc2 == 0), stop=(lc2 == LC - 1))
                if tt % 4 == 3:
                    last.then_inc(c.i("pe"), 1)

        @c.sec("vector")
        def _(v):
            for qt in range(4):
                v.wait_ge(c.s("pe"), pev + qt + 1)
                v.tensor_copy(vg[:, qt * 2048:(qt + 1) * 2048],
                              pA[:, :]).then_inc(c.i("dv"), 1)

        for hg in range(4):
            h = grp * 4 + hg
            _l0_qk(c, wq_d, D, h * 128, D, lnb, DC, q_sb, wb2)
            _l0_qk(c, wk_d, D, h * 128, LAT, None, LC, k_sb, wb2, lat=lat)
            emit_norm_max(c, q_sb, sqr, sc[0:1, 0:1])
            emit_norm_max(c, k_sb, sqr, sc[0:1, 1:2])
            emit_bias_from_norms(c, sc[0:1, 0:1], sc[0:1, 1:2])
            emit_attn_core(c, q_sb, k_sb, vg,
                           (lambda jt, hg=hg: jt * 512 + hg * 128),
                           atv[h, :, :], causal=False)


def emit_wo_residual(c, wo_d, at_scr, x_in, x_out):
    pA, pB = c.b["pA"], c.b["pB"]
    lnb = c.b["lnb"]
    WB, BIG0, BIG1 = c.b["WB"], c.b["BIG0"], c.b["BIG1"]
    b0f = BIG0.bitcast(F32)
    b1f = BIG1.bitcast(F32)
    xs = [b0f[:, 0:T], b0f[:, T:2 * T]]
    outf = [b1f[:, 0:T], b1f[:, T:2 * T]]
    wb2 = BIG1[:, 8192:12288]
    atv = at_scr.rearrange("(h p) t -> h p t", p=128)
    xiv = x_in.rearrange("(cc p) t -> cc p t", p=128)
    xov = x_out.rearrange("(cc p) t -> cc p t", p=128)
    dv0, ld0 = c.n["dv"], c.n["ld"]

    @c.sec("sync")
    def _(q):
        q.wait_ge(c.s("dv"), dv0)
        for hh in range(H):
            q.dma_start(out=lnb[:, hh * T:(hh + 1) * T],
                        in_=atv[hh, :, :]).then_inc(c.i("ld", 16), 16)

    st_base = {}
    for dm in range(DC):
        wdst = WB if dm % 2 == 0 else wb2
        pdst = pA if dm % 2 == 0 else pB

        @c.sec("gpsimd")
        def _(g, wdst=wdst, dm=dm):
            g.wait_ge(c.s("dv"), dv0)
            if dm >= 2:
                g.wait_ge(c.s("pe"), c.n["pe"] - 1)
            g.dma_start(out=wdst[:, 0:4096],
                        in_=wslice_ap(wo_d, D, dm * 128, 128, D)
                        ).then_inc(c.i("lg", 16), 16)

        ldx = c.n["ld"]

        @c.sec("sync")
        def _(q, dm=dm):
            if dm >= 2:
                q.wait_ge(c.s("dv"), dv0 + dm - 1)
            q.dma_start(out=xs[dm % 2][:, :],
                        in_=xiv[dm, :, :]).then_inc(c.i("ld", 16), 16)

        lgw, pew = c.n["lg"], c.n["pe"]

        @c.sec("tensor")
        def _(t, wdst=wdst, pdst=pdst, dm=dm):
            t.wait_ge(c.s("lg"), lgw)
            if dm == 0:
                t.wait_ge(c.s("ld"), ld0 + 16 * H)
            if dm >= 2:
                t.wait_ge(c.s("dv"), dv0 + dm - 1)
            last = None
            for ch in range(H):
                last = mm_tok(t, pdst, wdst[:, ch * 128:(ch + 1) * 128],
                              lnb, ch == 0, ch == H - 1, coff=ch * T)
            last.then_inc(c.i("pe"), 1)

        @c.sec("vector")
        def _(v, pdst=pdst, dm=dm):
            v.wait_ge(c.s("pe"), pew + 1)
            v.wait_ge(c.s("ld"), ldx + 16)
            if dm >= 2:
                v.wait_ge(c.s("ls"), st_base[dm - 2] + 16)
            v.tensor_tensor(outf[dm % 2], xs[dm % 2][:, :], pdst[:, :],
                            op=ALU.add).then_inc(c.i("dv"), 1)

        dvw = c.n["dv"]
        st_base[dm] = c.n["ls"]

        @c.sec("scalar")
        def _(s, dm=dm):
            s.wait_ge(c.s("dv"), dvw)
            s.dma_start(out=xov[dm, :, :],
                        in_=outf[dm % 2]).then_inc(c.i("ls", 16), 16)


def emit_attn_l1(c, wq_d, wk_d, wv_d, at_scr):
    pA = c.b["pA"]
    lnb = c.b["lnb"]
    BIG0, BIG1, WB = c.b["BIG0"], c.b["BIG1"], c.b["WB"]
    q_sb = BIG0[:, 8192:10240]
    k_sb = BIG0[:, 10240:12288]
    vg = BIG1[:, 0:2048]
    wb2 = BIG1[:, 8192:12288]
    sqr = c.b["WA"][:, 0:2048]
    sc = c.b["sc1"]
    atv = at_scr.rearrange("(h p) t -> h p t", p=128)
    dv0 = c.n["dv"]

    @c.sec("gpsimd")
    def _(g):
        g.wait_ge(c.s("dv"), dv0)
        g.dma_start(out=WB[:, 0:4096],
                    in_=wslice_ap(wk_d, HD, 0, HD, D)).then_inc(c.i("lg", 16), 16)
        g.dma_start(out=wb2[:, 0:4096],
                    in_=wslice_ap(wv_d, HD, 0, HD, D)).then_inc(c.i("lg", 16), 16)

    lg0, pe0 = c.n["lg"], c.n["pe"]

    @c.sec("tensor")
    def _(t):
        t.wait_ge(c.s("lg"), lg0 - 16)
        t.wait_ge(c.s("dv"), dv0)
        last = None
        for ch in range(DC):
            last = mm_tok(t, pA, WB[:, ch * 128:(ch + 1) * 128], lnb,
                          ch == 0, ch == DC - 1, coff=ch * T)
        last.then_inc(c.i("pe"), 1)

    @c.sec("vector")
    def _(v):
        v.wait_ge(c.s("pe"), pe0 + 1)
        v.tensor_copy(k_sb[:, :], pA[:, :]).then_inc(c.i("dv"), 1)

    pev, dvv = c.n["pe"], c.n["dv"]

    @c.sec("tensor")
    def _(t):
        t.wait_ge(c.s("lg"), lg0)
        t.wait_ge(c.s("dv"), dvv)
        for tt in range(16):
            if tt >= 4:
                t.wait_ge(c.s("dv"), dvv + tt - 3)
            last = None
            for ch in range(DC):
                last = t.matmul(
                    pA[:, (tt % 4) * 512:(tt % 4) * 512 + 128],
                    lnb[:, ch * T + tt * 128:ch * T + (tt + 1) * 128],
                    wb2[:, ch * 128:(ch + 1) * 128],
                    start=(ch == 0), stop=(ch == DC - 1))
            last.then_inc(c.i("pe"), 1)

    @c.sec("vector")
    def _(v):
        for tt in range(16):
            v.wait_ge(c.s("pe"), pev + tt + 1)
            v.tensor_copy(vg[:, tt * 128:(tt + 1) * 128],
                          pA[:, (tt % 4) * 512:(tt % 4) * 512 + 128]
                          ).then_inc(c.i("dv"), 1)

    emit_norm_max(c, k_sb, sqr, sc[0:1, 1:2])

    for h in range(4 * ATTN_GROUPS):
        dvq = c.n["dv"]

        @c.sec("gpsimd")
        def _(g):
            g.wait_ge(c.s("dv"), dvq)
            g.dma_start(out=wb2[:, 0:4096],
                        in_=wslice_ap(wq_d, D, h * 128, 128, D)
                        ).then_inc(c.i("lg", 16), 16)

        lgq, peq = c.n["lg"], c.n["pe"]

        @c.sec("tensor")
        def _(t):
            t.wait_ge(c.s("lg"), lgq)
            t.wait_ge(c.s("dv"), dvq)
            last = None
            for ch in range(DC):
                last = mm_tok(t, pA, wb2[:, ch * 128:(ch + 1) * 128],
                              lnb, ch == 0, ch == DC - 1, coff=ch * T)
            last.then_inc(c.i("pe"), 1)

        @c.sec("vector")
        def _(v):
            v.wait_ge(c.s("pe"), peq + 1)
            v.tensor_copy(q_sb[:, :], pA[:, :]).then_inc(c.i("dv"), 1)

        emit_norm_max(c, q_sb, sqr, sc[0:1, 0:1])
        emit_bias_from_norms(c, sc[0:1, 0:1], sc[0:1, 1:2])
        emit_attn_core(c, q_sb, k_sb, vg, (lambda jt: jt * 128),
                       atv[h, :, :], causal=True)


def emit_gs(c):
    pA = c.b["pA"]
    lnb = c.b["lnb"]
    mast = c.b["MB"]
    dv0 = c.n["dv"]

    @c.sec("gpsimd")
    def _(g):
        g.memset(mast[:, :], 1.0).then_inc(c.i("gp"), 1)
        g.wait_ge(c.s("gp"), c.n["gp"])
        g.affine_select(mast[:, :], mast[:, :], pattern=[[1, 128]],
                        compare_op=ALU.is_equal, fill=0.0, base=-31,
                        channel_multiplier=0).then_inc(c.i("gp"), 1)

    gp0 = c.n["gp"]

    @c.sec("tensor")
    def _(t):
        t.wait_ge(c.s("gp"), gp0)
        t.wait_ge(c.s("dv"), dv0)
        last = None
        for cc in range(DC):
            last = mm_tok(t, pA[0:32, :], mast[:, 31 - cc:63 - cc],
                          lnb, cc == 0, cc == DC - 1, coff=cc * T)
        last.then_inc(c.i("pe"), 1)

    pe0 = c.n["pe"]

    @c.sec("vector")
    def _(v):
        v.wait_ge(c.s("pe"), pe0)
        v.tensor_copy(c.b["GS"][:, :], pA[0:32, :]).then_inc(c.i("dv"), 1)


def emit_gateup(c, qwgu_d, qzgu_d, sgu_d, h_scr):
    pA, pB = c.b["pA"], c.b["pB"]
    lnb = c.b["lnb"]
    GS, QZ, SZ, FF0 = c.b["GS"], c.b["QZ"], c.b["SZ"], c.b["FF0"]
    WA, WB = c.b["WA"], c.b["WB"]
    BIG0, BIG1 = c.b["BIG0"], c.b["BIG1"]
    pkG = BIG0.bitcast(I32)
    pkU = BIG1.bitcast(I32)
    sbG = BIG0[:, 8192:12288]
    sbU = BIG1[:, 8192:12288]
    sg = FF0[:, 0:2048]
    hb = FF0[:, 2048:4096]
    gs_dv = c.n["dv"]

    for c_hi in range(11):
        w = 128 if c_hi < 10 else 96
        ld0 = c.n["ld"]

        @c.sec("sync")
        def _(q, c_hi=c_hi, w=w):
            if c_hi == 0:
                q.wait_ge(c.s("dv"), gs_dv)
            if c_hi >= 1:
                q.wait_ge(c.s("pe"), c.n["pe"])
            q.dma_start(out=pkG[:, 0:32 * w],
                        in_=wslice_ap(qwgu_d, 2752, c_hi * 128, w, D)
                        ).then_inc(c.i("ld", 16), 16)
            q.dma_start(out=pkU[:, 0:32 * w],
                        in_=wslice_ap(qwgu_d, 2752, 1376 + c_hi * 128, w, D)
                        ).then_inc(c.i("ld", 16), 16)
            q.dma_start(out=QZ[:, 0:w],
                        in_=bass.AP(qzgu_d.tensor, c_hi * 128,
                                    [[2752, 32], [1, 1], [1, w]])
                        ).then_inc(c.i("ld", 16), 16)
            q.dma_start(out=QZ[:, 128:128 + w],
                        in_=bass.AP(qzgu_d.tensor, 1376 + c_hi * 128,
                                    [[2752, 32], [1, 1], [1, w]])
                        ).then_inc(c.i("ld", 16), 16)

        for k in range(8):
            base = NHB[c_hi * 8 + k][0]
            lg0 = c.n["lg"]

            lg0 = c.n["lg"]
            dv_prev = c.n["dv"]

            @c.sec("gpsimd")
            def _(g, base=base, w=w, k=k, c_hi=c_hi):
                if c_hi == 0 and k == 0:
                    g.wait_ge(c.s("dv"), gs_dv)
                g.wait_ge(c.s("dv"), dv_prev)
                for side in range(2):
                    big = BIG0 if side == 0 else BIG1
                    soff = (0 if side == 0 else INTER) + base
                    for q in range(4):
                        if q >= 2:
                            g.wait_ge(c.s("dv"),
                                      dv_prev + 4 * side + q - 1)
                        g.dma_start(
                            out=big[:, 8192 + (q % 2) * 1024:
                                    8192 + (q % 2) * 1024 + 8 * w],
                            in_=bass.AP(sgu_d.tensor,
                                        soff + 8 * q * 22016,
                                        [[0, 128], [22016, 8], [1, w]])
                            ).then_inc(c.i("lg", 16), 16)
                g.dma_start(out=SZ[:, 0:w],
                            in_=bass.AP(sgu_d.tensor, base,
                                        [[22016, 32], [1, 1], [1, w]])
                            ).then_inc(c.i("lg", 16), 16)
                g.dma_start(out=SZ[:, 128:128 + w],
                            in_=bass.AP(sgu_d.tensor, INTER + base,
                                        [[22016, 32], [1, 1], [1, w]])
                            ).then_inc(c.i("lg", 16), 16)

            dvk = c.n["dv"]

            @c.sec("vector")
            def _(v, k=k, w=w, c_hi=c_hi):
                v.wait_ge(c.s("ld"), ld0 + 64)
                v.wait_ge(c.s("pe"), c.n["pe"])
                for side in range(2):
                    pk = pkG if side == 0 else pkU
                    wout = WA if side == 0 else WB
                    scr = (BIG0 if side == 0 else
                           BIG1).bitcast(I32)[:, 5120:6144]
                    for q in range(4):
                        v.wait_ge(c.s("lg"),
                                  lg0 + 16 * (4 * side + q + 1))
                        sq = (BIG0 if side == 0 else BIG1)[
                            :, 8192 + (q % 2) * 1024:
                            8192 + (q % 2) * 1024 + 8 * w]
                        v.tensor_scalar(scr[:, 0:8 * w],
                                        pk[:, 8 * q * w:(8 * q + 8) * w],
                                        4 * k, 15,
                                        op0=ALU.logical_shift_right,
                                        op1=ALU.bitwise_and)
                        v.tensor_tensor(wout[:, 8 * q * w:(8 * q + 8) * w],
                                        scr[:, 0:8 * w], sq,
                                        op=ALU.mult).then_inc(c.i("dv"), 1)
                v.wait_ge(c.s("lg"), lg0 + 160)
                zscr = BIG0.bitcast(I32)[0:32, 5120:5120 + 128]
                v.tensor_scalar(zscr[:, 0:w], QZ[:, 0:w], 4 * k, 15,
                                op0=ALU.logical_shift_right,
                                op1=ALU.bitwise_and)
                v.scalar_tensor_tensor(SZ[:, 256:256 + w], zscr[:, 0:w],
                                       -1.0, SZ[:, 0:w], op0=ALU.mult,
                                       op1=ALU.mult)
                v.tensor_scalar(zscr[:, 0:w], QZ[:, 128:128 + w],
                                4 * k, 15, op0=ALU.logical_shift_right,
                                op1=ALU.bitwise_and)
                v.scalar_tensor_tensor(SZ[:, 384:384 + w], zscr[:, 0:w],
                                       -1.0, SZ[:, 128:128 + w],
                                       op0=ALU.mult,
                                       op1=ALU.mult).then_inc(c.i("dv"), 1)

            peg = c.n["pe"]

            @c.sec("tensor")
            def _(t, w=w, ac_prev=c.n["ac"]):
                t.wait_ge(c.s("dv"), dvk + 9)
                t.wait_ge(c.s("ac"), ac_prev)
                t.wait_ge(c.s("dv"), gs_dv)
                for cc in range(DC):
                    mm_tok(t, pA[0:w, :] if w < 128 else pA,
                           WA[:, cc * w:(cc + 1) * w], lnb, cc == 0, False,
                           coff=cc * T)
                mm_tok(t, pA[0:w, :] if w < 128 else pA,
                       SZ[0:32, 256:256 + w], GS[0:32, :], False,
                       True).then_inc(c.i("pe"), 1)
                for cc in range(DC):
                    mm_tok(t, pB[0:w, :] if w < 128 else pB,
                           WB[:, cc * w:(cc + 1) * w], lnb, cc == 0, False,
                           coff=cc * T)
                mm_tok(t, pB[0:w, :] if w < 128 else pB,
                       SZ[0:32, 384:384 + w], GS[0:32, :], False,
                       True).then_inc(c.i("pe"), 1)

            ach = c.n["ac"]

            @c.sec("scalar")
            def _(s, w=w):
                s.wait_ge(c.s("pe"), peg + 1)
                s.activation(sg[0:w if w < 128 else 128, :],
                             pA[0:w, :] if w < 128 else pA[:, :],
                             AF.Silu, bias=0.0, scale=1.0
                             ).then_inc(c.i("ac"), 1)

            dvh = c.n["dv"]

            @c.sec("vector")
            def _(v, w=w):
                v.wait_ge(c.s("ac"), ach + 1)
                v.wait_ge(c.s("pe"), peg + 2)
                v.tensor_tensor(hb[0:w if w < 128 else 128, :],
                                sg[0:w if w < 128 else 128, :],
                                pB[0:w, :] if w < 128 else pB[:, :],
                                op=ALU.mult).then_inc(c.i("dv"), 1)

            @c.sec("sync")
            def _(q, base=base, w=w):
                q.wait_ge(c.s("dv"), dvh + 1)
                q.dma_start(out=bass.AP(h_scr.tensor, base * T,
                                        [[T, w], [1, 1], [1, T]]),
                            in_=hb[0:w if w < 128 else 128, :]
                            ).then_inc(c.i("ld", 16), 16)


def emit_wdn_dequant(c, qwdn_d, sdn_d, wdn4):
    BIG0 = c.b["BIG0"]
    pk = BIG0.bitcast(I32)
    sb = BIG0[:, 8192:12288]
    WA, WB = c.b["WA"], c.b["WB"]
    pe_entry, dv_entry = c.n["pe"], c.n["dv"]
    for c_hi in range(11):
        w = 128 if c_hi < 10 else 96
        ng = 8 if c_hi < 10 else 6
        lg0 = c.n["lg"]

        @c.sec("gpsimd")
        def _(g, c_hi=c_hi, ng=ng):
            if c_hi == 0:
                g.wait_ge(c.s("pe"), pe_entry)
                g.wait_ge(c.s("dv"), dv_entry)
            if c_hi >= 1:
                g.wait_ge(c.s("dv"), c.n["dv"])
            for g2 in range(ng):
                g.dma_start(
                    out=sb[16 * g2:16 * (g2 + 1), 0:4096],
                    in_=bass.AP(sdn_d.tensor, (8 * c_hi + g2) * 4096,
                                [[0, 16], [1, 1], [1, 4096]])
                    ).then_inc(c.i("lg", 16), 16)

        for k in range(8):
            ch = c_hi * 8 + k
            base = NHB[ch][0]
            buf = pk[:, 0:512] if k % 2 == 0 else pk[:, 512:1024]
            ld0 = c.n["ld"]

            @c.sec("sync")
            def _(q, base=base, w=w, buf=buf, k=k, c_hi=c_hi):
                if c_hi == 0 and k == 0:
                    q.wait_ge(c.s("pe"), pe_entry)
                    q.wait_ge(c.s("dv"), dv_entry)
                if k < 2:
                    q.wait_ge(c.s("dv"), c.n["dv"])
                if k >= 2:
                    q.wait_ge(c.s("dv"), c.n["dv"] - 8)
                q.dma_start(out=buf[0:w, :],
                            in_=bass.AP(qwdn_d.tensor, base * 512,
                                        [[512, w], [1, 1], [1, 512]])
                            ).then_inc(c.i("ld", 16), 16)

            dv0 = c.n["dv"]
            ls0 = c.n["ls"]

            for k2 in range(8):
                @c.sec("vector")
                def _(v, w=w, buf=buf, k2=k2):
                    if k2 == 0:
                        v.wait_ge(c.s("ld"), ld0 + 16)
                        v.wait_ge(c.s("lg"), lg0 + 16 * ng)
                    if k2 < 2:
                        v.wait_ge(c.s("ls"), ls0)
                    if k2 >= 2:
                        v.wait_ge(c.s("ls"), ls0 + 16 * (k2 - 1))
                    dst = WA[:, 0:512] if k2 % 2 == 0 else WA[:, 512:1024]
                    wscr = WA.bitcast(I32)[:, 1024:1536]
                    v.tensor_scalar(wscr[0:w, :], buf[0:w, :], 4 * k2, 15,
                                    op0=ALU.logical_shift_right,
                                    op1=ALU.bitwise_and)
                    v.tensor_tensor(dst[0:w, :], wscr[0:w, :],
                                    sb[0:w, k2 * 512:(k2 + 1) * 512],
                                    op=ALU.mult).then_inc(c.i("dv"), 1)

                @c.sec("scalar")
                def _(s, w=w, k2=k2, ch=ch):
                    s.wait_ge(c.s("dv"), dv0 + k2 + 1)
                    src = WA[:, 0:512] if k2 % 2 == 0 else WA[:, 512:1024]
                    s.dma_start(
                        out=bass.AP(wdn4.tensor,
                                    (k2 * 4) * (128 * 11264) + ch * 128,
                                    [[11264, w], [128 * 11264, 4],
                                     [1, 128]]),
                        in_=src[0:w, :]).then_inc(c.i("ls", 16), 16)


def emit_down(c, x_in, x_out, h_scr, wdn4, qzdn_d, sdn_d):
    pA, pB = c.b["pA"], c.b["pB"]
    lnb = c.b["lnb"]
    BIG0, BIG1 = c.b["BIG0"], c.b["BIG1"]
    WA, WB = c.b["WA"], c.b["WB"]
    SD, ZSD, QZD, HS = c.b["SD"], c.b["ZSD"], c.b["QZD"], c.b["HS"]
    FF0 = c.b["FF0"]
    ff = FF0.bitcast(F32)
    mast2 = c.b["MB2"]
    dvp = c.n["dv"]

    @c.sec("gpsimd")
    def _(g):
        g.wait_ge(c.s("pe"), c.n["pe"])
        g.memset(mast2[:, :], 1.0).then_inc(c.i("gp"), 1)
        g.wait_ge(c.s("gp"), c.n["gp"])
        g.affine_select(mast2[:, :], mast2[:, :], pattern=[[16, 176]],
                        compare_op=ALU.is_ge, fill=0.0,
                        base=-16 * 80 + 15, channel_multiplier=-1
                        ).then_inc(c.i("gp"), 1)
        g.wait_ge(c.s("gp"), c.n["gp"])
        g.affine_select(mast2[:, :], mast2[:, :], pattern=[[-16, 176]],
                        compare_op=ALU.is_ge, fill=0.0,
                        base=16 * 80, channel_multiplier=1
                        ).then_inc(c.i("gp"), 1)

    mb2_gp = c.n["gp"]

    ld0 = c.n["ld"]

    @c.sec("sync")
    def _(q):
        q.wait_ge(c.s("dv"), dvp)
        q.dma_start(out=QZD[:, :],
                    in_=bass.AP(qzdn_d.tensor, 0,
                                [[512, 86], [1, 1], [1, 512]])
                    ).then_inc(c.i("ld", 16), 16)

    @c.sec("gpsimd")
    def _(g):
        g.wait_ge(c.s("dv"), dvp)
        g.dma_start(out=SD[:, :],
                    in_=bass.AP(sdn_d.tensor, 0,
                                [[4096, 86], [1, 1], [1, 4096]])
                    ).then_inc(c.i("lg", 16), 16)

    lgz = c.n["lg"]
    dvz0 = c.n["dv"]

    @c.sec("vector")
    def _(v):
        v.wait_ge(c.s("ld"), ld0 + 16)
        v.wait_ge(c.s("lg"), lgz)
        zdscr = c.b["BIG0"].bitcast(I32)[0:86, 0:512]
        last = None
        for k2 in range(8):
            v.tensor_scalar(zdscr[:, :], QZD[:, :],
                            4 * k2, 15, op0=ALU.logical_shift_right,
                            op1=ALU.bitwise_and)
            last = v.scalar_tensor_tensor(
                ZSD[:, k2 * 512:(k2 + 1) * 512], zdscr[:, :], -1.0,
                SD[:, k2 * 512:(k2 + 1) * 512], op0=ALU.mult,
                op1=ALU.mult)
        last.then_inc(c.i("dv"), 1)

    zs_dv = c.n["dv"]
    xiv = x_in
    xov = x_out

    for tt in range(4):
        ldh = c.n["ld"]

        @c.sec("sync")
        def _(q, tt=tt):
            q.wait_ge(c.s("pe"), c.n["pe"])
            q.dma_start(out=lnb[:, 0:80 * 512],
                        in_=bass.AP(h_scr.tensor, tt * 512,
                                    [[T, 128], [128 * T, 80], [1, 512]])
                        ).then_inc(c.i("ld", 16), 16)
            q.dma_start(out=lnb[0:96, 80 * 512:88 * 512],
                        in_=bass.AP(h_scr.tensor, 10240 * T + tt * 512,
                                    [[T, 96], [96 * T, 8], [1, 512]])
                        ).then_inc(c.i("ld", 16), 16)

        peh = c.n["pe"]

        @c.sec("tensor")
        def _(t, tt=tt):
            t.wait_ge(c.s("ld"), ldh + 32)
            t.wait_ge(c.s("dv"), zs_dv)
            t.wait_ge(c.s("gp"), mb2_gp)
            last = None
            for ch in range(88):
                w = NHB[ch][1]
                last = t.matmul(pA[0:86, 0:512],
                                mast2[0:w, 80 - 8 * NHB[ch][2]:
                                      80 - 8 * NHB[ch][2] + 86],
                                lnb[0:w, ch * 512:(ch + 1) * 512],
                                start=(ch == 0), stop=(ch == 87))
            last.then_inc(c.i("pe"), 1)

        dvh = c.n["dv"]

        @c.sec("vector")
        def _(v):
            v.wait_ge(c.s("pe"), peh + 1)
            v.tensor_copy(HS[:, :], pA[0:86, 0:512]).then_inc(c.i("dv"), 1)

        for ot in range(32):
            k2, c2hi = ot // 4, ot % 4
            wbuf = BIG0 if ot % 2 == 0 else BIG1
            lgw = c.n["lg"]

            @c.sec("gpsimd")
            def _(g, ot=ot, wbuf=wbuf, tt=tt):
                if tt == 0 and ot < 2:
                    g.wait_ge(c.s("dv"), zs_dv)
                if ot >= 2 or tt > 0:
                    g.wait_ge(c.s("pe"), c.n["pe"] - 1)
                g.dma_start(out=wbuf[:, 0:11264],
                            in_=bass.AP(wdn4.tensor, ot * (128 * 11264),
                                        [[11264, 128], [1, 1], [1, 11264]])
                            ).then_inc(c.i("lg", 16), 16)

            ldx = c.n["ld"]
            row0 = 1024 * c2hi + k2

            @c.sec("sync")
            def _(q, row0=row0, tt=tt, ot=ot):
                q.wait_ge(c.s("dv"), zs_dv)
                if ot >= 2 or tt > 0:
                    q.wait_ge(c.s("dv"), c.n["dv"] - 1)
                q.dma_start(out=ff[:, 0:512] if ot % 2 == 0
                            else ff[:, 512:1024],
                            in_=bass.AP(xiv.tensor, row0 * T + tt * 512,
                                        [[8 * T, 128], [1, 1], [1, 512]])
                            ).then_inc(c.i("ld", 16), 16)

            ped = c.n["pe"]

            @c.sec("tensor")
            def _(t, ot=ot, wbuf=wbuf):
                t.wait_ge(c.s("lg"), lgw + 16)
                t.wait_ge(c.s("dv"), dvh + 1)
                if ot >= 2 or tt > 0:
                    t.wait_ge(c.s("dv"), c.n["dv"] - 1)
                ps = pA[:, 512:1024] if ot % 2 == 0 else pA[:, 1024:1536]
                for ch in range(88):
                    w = NHB[ch][1]
                    t.matmul(ps, wbuf[0:w, ch * 128:(ch + 1) * 128],
                             lnb[0:w, ch * 512:(ch + 1) * 512],
                             start=(ch == 0), stop=False)
                t.matmul(ps, ZSD[:, k2 * 512 + c2hi * 128:
                                 k2 * 512 + c2hi * 128 + 128],
                         HS[:, :], start=False,
                         stop=True).then_inc(c.i("pe"), 1)

            dvd = c.n["dv"]

            @c.sec("vector")
            def _(v, ot=ot):
                v.wait_ge(c.s("pe"), ped + 1)
                v.wait_ge(c.s("ld"), ldx + 16)
                ps = pA[:, 512:1024] if ot % 2 == 0 else pA[:, 1024:1536]
                xf = ff[:, 0:512] if ot % 2 == 0 else ff[:, 512:1024]
                of = ff[:, 1024:1536] if ot % 2 == 0 else ff[:, 1536:2048]
                v.tensor_tensor(of, xf, ps, op=ALU.add).then_inc(
                    c.i("dv"), 1)

            @c.sec("scalar")
            def _(s, row0=row0, tt=tt, ot=ot):
                s.wait_ge(c.s("dv"), dvd + 1)
                of = ff[:, 1024:1536] if ot % 2 == 0 else ff[:, 1536:2048]
                s.dma_start(out=bass.AP(xov.tensor, row0 * T + tt * 512,
                                        [[8 * T, 128], [1, 1], [1, 512]]),
                            in_=of).then_inc(c.i("ls", 16), 16)


def emit_ffn_l1(c, wbT_d, bbT_d, wrT_d, acT_d, bcT_d, x_in, x_out):
    pA, pB = c.b["pA"], c.b["pB"]
    lnb = c.b["lnb"]
    BIG0, BIG1 = c.b["BIG0"], c.b["BIG1"]
    WA, WB = c.b["WA"], c.b["WB"]
    r8 = c.b["r8"]
    b0f = BIG0.bitcast(F32)
    b1f = BIG1.bitcast(F32)
    h_all = BIG0[:, 0:2048]
    g_all = BIG0[:, 2048:4096]
    lg_tm = b0f[:, 2048:2176]
    w_tm = b0f[:, 2176:2304]
    scol = [b0f[:, 2304 + i:2305 + i] for i in range(6)]
    e_scr = b0f[:, 2312:2320]
    w_fm = BIG0[0:8, 4736:6784]
    wb2 = BIG0[:, 8192:12288]
    xs = [b1f[:, 0:T], b1f[:, 0:T]]
    outf = [b1f[:, T:2 * T], b1f[:, 2 * T:3 * T]]
    IND, BB, identf = c.b["IND"], c.b["BB"], c.b["identf"]
    xiv = x_in.rearrange("(cc p) t -> cc p t", p=128)
    xov = x_out.rearrange("(cc p) t -> cc p t", p=128)
    dv0 = c.n["dv"]
    lg0 = c.n["lg"]

    @c.sec("gpsimd")
    def _(g):
        g.wait_ge(c.s("dv"), dv0)
        g.memset(identf[:, :], 1.0).then_inc(c.i("gp"), 1)
        g.wait_ge(c.s("gp"), c.n["gp"])
        g.affine_select(identf[:, :], identf[:, :], pattern=[[-1, 128]],
                        compare_op=ALU.is_equal, fill=0.0, base=0,
                        channel_multiplier=1).then_inc(c.i("gp"), 1)
        g.dma_start(out=WB[:, 0:4096],
                    in_=wslice_ap(acT_d, 128, 0, 128, D)
                    ).then_inc(c.i("lg", 16), 16)
        g.dma_start(out=WA[:, 0:4096],
                    in_=wslice_ap(bcT_d, 4096, 0, 4096, 128)
                    ).then_inc(c.i("lg", 16), 16)
        g.dma_start(out=wb2[:, 0:256],
                    in_=wslice_ap(wrT_d, 8, 0, 8, D)).then_inc(c.i("lg", 16), 16)

    @c.sec("sync")
    def _(q):
        q.dma_start(out=BB[:, :],
                    in_=bass.AP(bbT_d.tensor, 0,
                                [[32, 128], [1, 1], [1, 32]])
                    ).then_inc(c.i("ld", 16), 16)
    ld_bb = c.n["ld"]

    pe0 = c.n["pe"]

    @c.sec("tensor")
    def _(t):
        t.wait_ge(c.s("lg"), lg0 + 16)
        t.wait_ge(c.s("dv"), dv0)
        last = None
        for ch in range(DC):
            last = mm_tok(t, pA, WB[:, ch * 128:(ch + 1) * 128], lnb,
                          ch == 0, ch == DC - 1, coff=ch * T)
        last.then_inc(c.i("pe"), 1)

    @c.sec("vector")
    def _(v):
        v.wait_ge(c.s("pe"), pe0 + 1)
        v.tensor_copy(h_all, pA[:, :]).then_inc(c.i("dv"), 1)

    dvh = c.n["dv"]

    @c.sec("tensor")
    def _(t):
        t.wait_ge(c.s("lg"), lg0 + 48)
        t.wait_ge(c.s("dv"), dvh)
        last = None
        for ch in range(DC):
            last = mm_tok(t, pA[0:8, :], wb2[:, ch * 8:(ch + 1) * 8], lnb,
                          ch == 0, ch == DC - 1, coff=ch * T)
        last.then_inc(c.i("pe"), 1)

    pel = c.n["pe"]

    @c.sec("vector")
    def _(v):
        v.wait_ge(c.s("pe"), pel)
        v.tensor_copy(r8, pA[0:8, :]).then_inc(c.i("dv"), 1)

    dvl = c.n["dv"]

    gp_id = c.n["gp"]

    @c.sec("tensor")
    def _(t):
        t.wait_ge(c.s("dv"), dvl)
        t.wait_ge(c.s("gp"), gp_id)
        last = None
        for tt2 in range(16):
            last = t.transpose(pB[:, tt2 * 8:(tt2 + 1) * 8],
                               r8[0:8, tt2 * 128:(tt2 + 1) * 128],
                               identf[0:8, 0:8])
        last.then_inc(c.i("pe"), 1)

    pet = c.n["pe"]

    @c.sec("vector")
    def _(v):
        v.wait_ge(c.s("pe"), pet)
        v.tensor_copy(lg_tm, pB[:, 0:128]).then_inc(c.i("dv"), 1)

    dvt = c.n["dv"]
    act = c.n["ac"]

    for tt2 in range(16):
        lg_s = lg_tm[:, tt2 * 8:(tt2 + 1) * 8]
        w_s = w_tm[:, tt2 * 8:(tt2 + 1) * 8]

        @c.sec("vector")
        def _(v, lg_s=lg_s, w_s=w_s, tt2=tt2):
            if tt2 == 0:
                v.wait_ge(c.s("dv"), dvt)
            v.reduce_max(scol[0], lg_s, axis=AX.X).then_inc(c.i("dv"), 1)
            v.wait_ge(c.s("dv"), c.n["dv"])
            v.tensor_scalar(scol[1], scol[0], -1.0, None, op0=ALU.mult)
            v.tensor_scalar(w_s, lg_s, scol[0][:, 0:1], None,
                            op0=ALU.is_ge).then_inc(c.i("dv"), 1)
            v.wait_ge(c.s("dv"), c.n["dv"])
            v.scalar_tensor_tensor(w_s, w_s, -1e30, lg_s,
                                   op0=ALU.mult,
                                   op1=ALU.add).then_inc(c.i("dv"), 1)
            v.wait_ge(c.s("dv"), c.n["dv"])
            v.reduce_max(scol[2], w_s, axis=AX.X).then_inc(c.i("dv"), 1)

        @c.sec("scalar")
        def _(s, lg_s=lg_s, tt2=tt2):
            s.wait_ge(c.s("dv"), dvt + 7 * tt2 + 4)
            s.activation(e_scr, lg_s, AF.Exp, bias=scol[1][:, 0:1],
                         scale=1.0)
            s.activation(scol[4], scol[2], AF.Exp, bias=scol[1][:, 0:1],
                         scale=1.0).then_inc(c.i("ac"), 1)
            if tt2 == 0 and DBG_T2:
                s.copy(b0f[:, 2320:2326], b0f[:, 2304:2310])
                s.copy(b0f[:, 2326:2334], e_scr).then_inc(c.i("ac"), 1)

        @c.sec("vector")
        def _(v, lg_s=lg_s, w_s=w_s, tt2=tt2):
            v.wait_ge(c.s("ac"), act + tt2 + 1 + (1 if DBG_T2 else 0))
            v.tensor_scalar(w_s, lg_s, scol[2][:, 0:1], None,
                            op0=ALU.is_ge)
            v.tensor_scalar(scol[4], scol[4], 1.0, None,
                            op0=ALU.add).then_inc(c.i("dv"), 1)
            v.wait_ge(c.s("dv"), c.n["dv"])
            v.reciprocal(scol[5], scol[4]).then_inc(c.i("dv"), 1)
            v.wait_ge(c.s("dv"), c.n["dv"])
            v.tensor_tensor(w_s, w_s, e_scr, op=ALU.mult)
            v.tensor_scalar(w_s, w_s, scol[5][:, 0:1], None,
                            op0=ALU.mult).then_inc(c.i("dv"), 1)

    dvw = c.n["dv"]

    @c.sec("tensor")
    def _(t):
        t.wait_ge(c.s("dv"), dvw)
        last = None
        for tt2 in range(16):
            last = t.transpose(pB[0:8, tt2 * 128:(tt2 + 1) * 128],
                               w_tm[:, tt2 * 8:(tt2 + 1) * 8],
                               identf[0:128, 0:128])
        last.then_inc(c.i("pe"), 1)

    pew = c.n["pe"]

    @c.sec("vector")
    def _(v):
        v.wait_ge(c.s("pe"), pew)
        v.tensor_copy(w_fm[0:8, :], pB[0:8, 0:2048]).then_inc(c.i("dv"), 1)

    dvf = c.n["dv"]

    @c.sec("tensor")
    def _(t):
        t.wait_ge(c.s("dv"), dvf)
        mm_tok(t, pA, IND[0:8, :], w_fm[0:8, :], True,
               True).then_inc(c.i("pe"), 1)

    peb = c.n["pe"]

    @c.sec("vector")
    def _(v):
        v.wait_ge(c.s("pe"), peb)
        v.tensor_tensor(g_all, h_all, pA[:, :],
                        op=ALU.mult).then_inc(c.i("dv"), 1)

    dvg = c.n["dv"]
    st_base = {}

    for dm in range(DC):
        wdst = WB if dm % 2 == 0 else wb2
        pdst = pA if dm % 2 == 0 else pB

        @c.sec("gpsimd")
        def _(g, wdst=wdst, dm=dm):
            if dm == 0:
                g.wait_ge(c.s("dv"), dvg)
            if dm >= 2:
                g.wait_ge(c.s("pe"), c.n["pe"] - 1)
            g.dma_start(out=wdst[:, 0:4096],
                        in_=wslice_ap(wbT_d, D, dm * 128, 128, D)
                        ).then_inc(c.i("lg", 16), 16)

        ldx = c.n["ld"]

        @c.sec("sync")
        def _(q, dm=dm):
            if dm == 0:
                q.wait_ge(c.s("dv"), dvg)
            if dm >= 1:
                q.wait_ge(c.s("dv"), dvg + dm)
            q.dma_start(out=xs[dm % 2][:, :],
                        in_=xiv[dm, :, :]).then_inc(c.i("ld", 16), 16)

        lgw, pebl = c.n["lg"], c.n["pe"]

        @c.sec("tensor")
        def _(t, wdst=wdst, pdst=pdst, dm=dm):
            t.wait_ge(c.s("lg"), lgw)
            t.wait_ge(c.s("dv"), dvg + max(0, dm - 1))
            for ch in range(DC):
                mm_tok(t, pdst, wdst[:, ch * 128:(ch + 1) * 128], lnb,
                       ch == 0, False, coff=ch * T)
            mm_tok(t, pdst, WA[:, dm * 128:(dm + 1) * 128], g_all,
                   False, True).then_inc(c.i("pe"), 1)

        @c.sec("vector")
        def _(v, pdst=pdst, dm=dm):
            v.wait_ge(c.s("pe"), pebl + 1)
            v.wait_ge(c.s("ld"), ldx + 16)
            if dm == 0:
                v.wait_ge(c.s("ld"), ld_bb)
            if dm >= 2:
                v.wait_ge(c.s("ls"), st_base[dm - 2] + 16)
            v.scalar_tensor_tensor(outf[dm % 2], pdst[:, :],
                                   BB[:, dm:dm + 1], xs[dm % 2][:, :],
                                   op0=ALU.add,
                                   op1=ALU.add).then_inc(c.i("dv"), 1)

        dvo = c.n["dv"]
        st_base[dm] = c.n["ls"]

        @c.sec("scalar")
        def _(s, dm=dm):
            s.wait_ge(c.s("dv"), dvo)
            s.dma_start(out=xov[dm, :, :],
                        in_=outf[dm % 2]).then_inc(c.i("ls", 16), 16)


def build_l0a():
    c = Ctx()
    nc = c.nc
    x = nc.dram_tensor("x", [D, T], F32, kind="ExternalInput").ap()
    wq = nc.dram_tensor("wq", [D, D], F32, kind="ExternalInput").ap()
    wc = nc.dram_tensor("wc", [D, LAT], F32, kind="ExternalInput").ap()
    wk = nc.dram_tensor("wk", [LAT, D], F32, kind="ExternalInput").ap()
    wv = nc.dram_tensor("wv", [LAT, D], F32, kind="ExternalInput").ap()
    wo = nc.dram_tensor("wo", [D, D], F32, kind="ExternalInput").ap()
    x1 = nc.dram_tensor("x1", [D, T], F32, kind="ExternalOutput").ap()
    at = nc.dram_tensor("at", [D, T], BF16, kind="Internal").ap()
    c.start()
    alloc_common(c, "attn0")
    emit_rope_tables(c)
    emit_ln(c, x)
    emit_attn_l0(c, wq, wc, wk, wv, at)
    emit_wo_residual(c, wo, at, x, x1)
    c.fin()
    return c


def build_l0f():
    c = Ctx()
    nc = c.nc
    x1 = nc.dram_tensor("x1", [D, T], F32, kind="ExternalInput").ap()
    qwgu = nc.dram_tensor("qwgu", [D, 2752], I32, kind="ExternalInput").ap()
    qzgu = nc.dram_tensor("qzgu", [32, 2752], I32, kind="ExternalInput").ap()
    sgu = nc.dram_tensor("sgu", [32, 2 * INTER], F32,
                         kind="ExternalInput").ap()
    qwdn = nc.dram_tensor("qwdn", [INTER, 512], I32,
                          kind="ExternalInput").ap()
    qzdn = nc.dram_tensor("qzdn", [86, 512], I32, kind="ExternalInput").ap()
    sdn = nc.dram_tensor("sdn", [86, D], F32, kind="ExternalInput").ap()
    x2 = nc.dram_tensor("x2", [D, T], F32, kind="ExternalOutput").ap()
    h_scr = nc.dram_tensor("h_scr", [INTER, T], BF16, kind="Internal").ap()
    wdn4 = nc.dram_tensor("wdn4", [4096, 11264], BF16, kind="Internal").ap()
    c.start()
    alloc_common(c, "ffn0")
    emit_ln(c, x1)
    emit_gs(c)
    emit_gateup(c, qwgu, qzgu, sgu, h_scr)
    emit_wdn_dequant(c, qwdn, sdn, wdn4)
    emit_down(c, x1, x2, h_scr, wdn4, qzdn, sdn)
    c.fin()
    return c


def build_l1a():
    c = Ctx()
    nc = c.nc
    x2 = nc.dram_tensor("x2", [D, T], F32, kind="ExternalInput").ap()
    wq = nc.dram_tensor("wq", [D, D], F32, kind="ExternalInput").ap()
    wk = nc.dram_tensor("wk", [D, HD], F32, kind="ExternalInput").ap()
    wv = nc.dram_tensor("wv", [D, HD], F32, kind="ExternalInput").ap()
    wo = nc.dram_tensor("wo", [D, D], F32, kind="ExternalInput").ap()
    x3 = nc.dram_tensor("x3", [D, T], F32, kind="ExternalOutput").ap()
    at = nc.dram_tensor("at", [D, T], BF16, kind="Internal").ap()
    c.start()
    alloc_common(c, "attn1")
    emit_ln(c, x2)
    emit_attn_l1(c, wq, wk, wv, at)
    emit_wo_residual(c, wo, at, x2, x3)
    c.fin()
    return c


def build_l1f():
    c = Ctx()
    nc = c.nc
    x3 = nc.dram_tensor("x3", [D, T], F32, kind="ExternalInput").ap()
    wbT = nc.dram_tensor("wbT", [D, D], F32, kind="ExternalInput").ap()
    bbT = nc.dram_tensor("bbT", [128, 32], F32, kind="ExternalInput").ap()
    wrT = nc.dram_tensor("wrT", [D, 8], F32, kind="ExternalInput").ap()
    acT = nc.dram_tensor("acT", [D, 128], F32, kind="ExternalInput").ap()
    bcT = nc.dram_tensor("bcT", [128, D], F32, kind="ExternalInput").ap()
    xo = nc.dram_tensor("xo", [D, T], F32, kind="ExternalOutput").ap()
    c.start()
    alloc_common(c, "ffn1")
    emit_ln(c, x3)
    emit_ffn_l1(c, wbT, bbT, wrT, acT, bcT, x3, xo)
    c.fin()
    return c


def _prep_weights(inputs):
    w = {}
    w["l0_wq"] = np.ascontiguousarray(inputs["L0_attn_Wq"])
    w["l0_wc"] = np.ascontiguousarray(inputs["L0_attn_Wc"])
    w["l0_wk"] = np.ascontiguousarray(inputs["L0_attn_Wk"])
    w["l0_wv"] = np.ascontiguousarray(inputs["L0_attn_Wv"])
    w["l0_wo"] = np.ascontiguousarray(inputs["L0_attn_Wo"])
    w["qwgu"] = np.ascontiguousarray(inputs["L0_ffn_qweight_gu"])
    w["qzgu"] = np.ascontiguousarray(inputs["L0_ffn_qzeros_gu"])
    s = inputs["L0_ffn_scales_gu"]
    w["sgu"] = np.ascontiguousarray(
        np.concatenate([s[:, NHAT], s[:, INTER + NHAT]], axis=1))
    w["qwdn"] = np.ascontiguousarray(inputs["L0_ffn_qweight_dn"][NHAT, :])
    w["qzdn"] = np.ascontiguousarray(inputs["L0_ffn_qzeros_dn"])
    w["sdn"] = np.ascontiguousarray(inputs["L0_ffn_scales_dn"][:, KHAT])
    w["l1_wq"] = np.ascontiguousarray(inputs["L1_attn_wq"])
    w["l1_wk"] = np.ascontiguousarray(inputs["L1_attn_wk"])
    w["l1_wv"] = np.ascontiguousarray(inputs["L1_attn_wv"])
    w["l1_wo"] = np.ascontiguousarray(inputs["L1_attn_wo"])
    w["wbT"] = np.ascontiguousarray(inputs["L1_ffn_W_base"].T)
    w["bbT"] = np.ascontiguousarray(
        inputs["L1_ffn_b_base"].reshape(32, 128).T)
    w["wrT"] = np.ascontiguousarray(inputs["L1_ffn_W_router"].T)
    w["acT"] = np.ascontiguousarray(
        inputs["L1_ffn_A"].reshape(128, D).T)
    w["bcT"] = np.ascontiguousarray(
        inputs["L1_ffn_B"].transpose(0, 2, 1).reshape(128, D))
    return w


def kernel(**inputs):
    return kernel_fused(inputs)


def kernel_multilaunch(**inputs):
    NCORES = 8
    x = np.asarray(inputs["L0_attn_hidden_states"])
    w = _prep_weights(inputs)
    xT = [np.ascontiguousarray(x[i].T) for i in range(NCORES)]

    c = build_l0a()
    maps = [dict(x=xT[i], wq=w["l0_wq"], wc=w["l0_wc"], wk=w["l0_wk"],
                 wv=w["l0_wv"], wo=w["l0_wo"]) for i in range(NCORES)]
    r = run_bass_kernel_spmd(c.nc, maps, core_ids=list(range(NCORES)))
    c.close()
    x1 = [r.results[i]["x1"] for i in range(NCORES)]

    c = build_l0f()
    maps = [dict(x1=x1[i], qwgu=w["qwgu"], qzgu=w["qzgu"], sgu=w["sgu"],
                 qwdn=w["qwdn"], qzdn=w["qzdn"], sdn=w["sdn"])
            for i in range(NCORES)]
    r = run_bass_kernel_spmd(c.nc, maps, core_ids=list(range(NCORES)))
    c.close()
    x2 = [r.results[i]["x2"] for i in range(NCORES)]

    c = build_l1a()
    maps = [dict(x2=x2[i], wq=w["l1_wq"], wk=w["l1_wk"], wv=w["l1_wv"],
                 wo=w["l1_wo"]) for i in range(NCORES)]
    r = run_bass_kernel_spmd(c.nc, maps, core_ids=list(range(NCORES)))
    c.close()
    x3 = [r.results[i]["x3"] for i in range(NCORES)]

    c = build_l1f()
    maps = [dict(x3=x3[i], wbT=w["wbT"], bbT=w["bbT"], wrT=w["wrT"],
                 acT=w["acT"], bcT=w["bcT"]) for i in range(NCORES)]
    r = run_bass_kernel_spmd(c.nc, maps, core_ids=list(range(NCORES)))
    c.close()
    out = np.stack([r.results[i]["xo"].T for i in range(NCORES)], axis=0)
    return np.ascontiguousarray(out.astype(np.float32))


def build_fused():
    c = Ctx()
    nc = c.nc
    x = nc.dram_tensor("x", [D, T], F32, kind="ExternalInput").ap()
    wq0 = nc.dram_tensor("wq0", [D, D], F32, kind="ExternalInput").ap()
    wc0 = nc.dram_tensor("wc0", [D, LAT], F32, kind="ExternalInput").ap()
    wk0 = nc.dram_tensor("wk0", [LAT, D], F32, kind="ExternalInput").ap()
    wv0 = nc.dram_tensor("wv0", [LAT, D], F32, kind="ExternalInput").ap()
    wo0 = nc.dram_tensor("wo0", [D, D], F32, kind="ExternalInput").ap()
    qwgu = nc.dram_tensor("qwgu", [D, 2752], I32, kind="ExternalInput").ap()
    qzgu = nc.dram_tensor("qzgu", [32, 2752], I32, kind="ExternalInput").ap()
    sgu = nc.dram_tensor("sgu", [32, 2 * INTER], F32,
                         kind="ExternalInput").ap()
    qwdn = nc.dram_tensor("qwdn", [INTER, 512], I32,
                          kind="ExternalInput").ap()
    qzdn = nc.dram_tensor("qzdn", [86, 512], I32, kind="ExternalInput").ap()
    sdn = nc.dram_tensor("sdn", [86, D], F32, kind="ExternalInput").ap()
    wq1 = nc.dram_tensor("wq1", [D, D], F32, kind="ExternalInput").ap()
    wk1 = nc.dram_tensor("wk1", [D, HD], F32, kind="ExternalInput").ap()
    wv1 = nc.dram_tensor("wv1", [D, HD], F32, kind="ExternalInput").ap()
    wo1 = nc.dram_tensor("wo1", [D, D], F32, kind="ExternalInput").ap()
    wbT = nc.dram_tensor("wbT", [D, D], F32, kind="ExternalInput").ap()
    bbT = nc.dram_tensor("bbT", [128, 32], F32, kind="ExternalInput").ap()
    wrT = nc.dram_tensor("wrT", [D, 8], F32, kind="ExternalInput").ap()
    acT = nc.dram_tensor("acT", [D, 128], F32, kind="ExternalInput").ap()
    bcT = nc.dram_tensor("bcT", [128, D], F32, kind="ExternalInput").ap()
    xo = nc.dram_tensor("xo", [D, T], F32, kind="ExternalOutput").ap()
    x1 = nc.dram_tensor("x1", [D, T], F32, kind="Internal").ap()
    x2 = nc.dram_tensor("x2", [D, T], F32, kind="Internal").ap()
    x3 = nc.dram_tensor("x3", [D, T], F32, kind="Internal").ap()
    at = nc.dram_tensor("at", [D, T], BF16, kind="Internal").ap()
    h_scr = nc.dram_tensor("h_scr", [INTER, T], BF16, kind="Internal").ap()
    wdn4 = nc.dram_tensor("wdn4", [4096, 11264], BF16, kind="Internal").ap()

    c.start()
    alloc_common(c, "fused")
    emit_rope_tables(c)
    emit_ln(c, x)
    c.barrier()
    emit_attn_l0(c, wq0, wc0, wk0, wv0, at)
    c.barrier()
    emit_wo_residual(c, wo0, at, x, x1)
    c.barrier()
    emit_ln(c, x1)
    c.barrier()
    emit_gs(c)
    c.barrier()
    emit_gateup(c, qwgu, qzgu, sgu, h_scr)
    c.barrier()
    emit_wdn_dequant(c, qwdn, sdn, wdn4)
    c.barrier()
    emit_down(c, x1, x2, h_scr, wdn4, qzdn, sdn)
    c.barrier()
    emit_ln(c, x2)
    c.barrier()
    emit_attn_l1(c, wq1, wk1, wv1, at)
    c.barrier()
    emit_wo_residual(c, wo1, at, x2, x3)
    c.barrier()
    emit_ln(c, x3)
    c.barrier()
    emit_ffn_l1(c, wbT, bbT, wrT, acT, bcT, x3, xo)
    c.fin()
    return c


def kernel_fused(inputs):
    NCORES = 8
    x = np.asarray(inputs["L0_attn_hidden_states"])
    w = _prep_weights(inputs)
    xT = [np.ascontiguousarray(x[i].T) for i in range(NCORES)]
    c = build_fused()
    maps = [dict(x=xT[i], wq0=w["l0_wq"], wc0=w["l0_wc"], wk0=w["l0_wk"],
                 wv0=w["l0_wv"], wo0=w["l0_wo"], qwgu=w["qwgu"],
                 qzgu=w["qzgu"], sgu=w["sgu"], qwdn=w["qwdn"],
                 qzdn=w["qzdn"], sdn=w["sdn"], wq1=w["l1_wq"],
                 wk1=w["l1_wk"], wv1=w["l1_wv"], wo1=w["l1_wo"],
                 wbT=w["wbT"], bbT=w["bbT"], wrT=w["wrT"], acT=w["acT"],
                 bcT=w["bcT"]) for i in range(NCORES)]
    r = run_bass_kernel_spmd(c.nc, maps, core_ids=list(range(NCORES)))
    c.close()
    out = np.stack([r.results[i]["xo"].T for i in range(NCORES)], axis=0)
    return np.ascontiguousarray(out.astype(np.float32))


# version 03
# speedup vs baseline: 1.0397x; 1.0397x over previous
import contextlib
import numpy as np
import ml_dtypes
import concourse.bass as bass
import concourse.mybir as mybir
from concourse.bass_utils import run_bass_kernel_spmd
from concourse.alu_op_type import AluOpType as ALU

F32 = mybir.dt.float32
BF16 = mybir.dt.bfloat16
I32 = mybir.dt.int32
AF = mybir.ActivationFunctionType
AX = mybir.AxisListType

T = 2048
D = 4096
DC = 32
H = 32
HD = 128
LAT = 512
LC = 4
INTER = 11008
EPS = 1e-5
ISCALE = float(1.0 / np.sqrt(HD))
PI = float(np.pi)
TWO_PI = float(2.0 * np.pi)


def _nhat_blocks():
    out = []
    for c_hi in range(11):
        w = 128 if c_hi < 10 else 96
        for k in range(8):
            base = (c_hi * 1024 + k * 128) if c_hi < 10 else (10240 + k * 96)
            out.append((base, w, c_hi, k))
    return out

NHB = _nhat_blocks()

def _nhat_perm():
    p = np.empty(INTER, dtype=np.int64)
    for base, w, c_hi, k in NHB:
        c = c_hi * 128 + np.arange(w)
        p[base:base + w] = 8 * c + k
    return p

def _khat_perm():
    p = np.empty(D, dtype=np.int64)
    for k2 in range(8):
        c2 = np.arange(512)
        p[k2 * 512 + c2] = 8 * c2 + k2
    return p

NHAT = _nhat_perm()
KHAT = _khat_perm()


class Ctx:
    def __init__(self, detect_races=True):
        self.nc = bass.Bass("TRN2", target_bir_lowering=False,
                            detect_race_conditions=detect_races)
        self.stack = contextlib.ExitStack()
        self.block = None
        self.n = {}
        self.sems = {}
        self.b = {}

    def start(self):
        for s in ["ld", "lg", "ls", "pe", "dv", "ac", "gp"]:
            self.sems[s] = self.stack.enter_context(self.nc.semaphore(s))
            self.n[s] = 0
        self.block = self.stack.enter_context(self.nc.Block())

    def sbuf(self, name, shape, dtype):
        t = self.stack.enter_context(self.nc.sbuf_tensor(name, shape, dtype))
        self.b[name] = t
        return t

    def psum(self, name, shape, dtype=F32):
        t = self.stack.enter_context(self.nc.psum_tensor(name, shape, dtype))
        self.b[name] = t
        return t

    def sec(self, eng):
        return getattr(self.block, eng)

    def i(self, sem, by=1):
        self.n[sem] += by
        return self.sems[sem]

    def s(self, sem):
        return self.sems[sem]

    def barrier(self):
        counts = dict(self.n)

        for eng in ["sync", "gpsimd", "vector", "scalar", "tensor"]:
            @self.sec(eng)
            def _(e):
                for sm in ["ld", "lg", "ls", "pe", "dv", "ac", "gp"]:
                    e.wait_ge(self.s(sm), counts[sm])

    def fin(self):
        nld, nlg, nls = self.n["ld"], self.n["lg"], self.n["ls"]

        @self.sec("sync")
        def _(q):
            q.wait_ge(self.s("ld"), nld)
            q.wait_ge(self.s("lg"), nlg)
            q.wait_ge(self.s("ls"), nls)

    def close(self):
        self.stack.close()


def mm_tok(t, pdst, lhsT, rhs, start, stop, poff=0, coff=0, cols=T):
    last = None
    for jb in range(0, cols, 512):
        last = t.matmul(pdst[:, poff + jb:poff + jb + 512], lhsT,
                        rhs[:, coff + jb:coff + jb + 512], start=start,
                        stop=stop)
    return last


def wslice_ap(w_ap, row_stride, col0, ncols, nrows):
    nch = nrows // 128
    return bass.AP(w_ap.tensor, col0,
                   [[row_stride, 128], [row_stride * 128, nch], [1, ncols]])


def alloc_common(c, kind):
    c.psum("pA", [128, T], F32)
    c.psum("pB", [128, T], F32)
    c.sbuf("lnb", [128, DC * T], BF16)
    c.sbuf("BIG0", [128, 12288], BF16)
    c.sbuf("BIG1", [128, 12288], BF16)
    c.sbuf("WA", [128, 4096], BF16)
    c.sbuf("WB", [128, 4096], BF16)
    c.b["rwb"] = c.b["WB"].bitcast(F32)[0:1, :]
    c.b["rwa"] = c.b["WA"].bitcast(F32)[0:1, 0:T]
    c.b["rb1"] = c.b["BIG1"].bitcast(F32)[0:1, 0:T]
    c.b["r8"] = c.b["WB"].bitcast(F32)[0:8, :]
    c.sbuf("ones_bf", [1, 128], BF16)
    c.sbuf("ones_cb", [128, 1], BF16)
    c.sbuf("ones_f", [1, 128], F32)
    c.sbuf("eps_t", [1, 1], F32)
    c.sbuf("bias_t", [128, 1], F32)
    c.sbuf("sc1", [4, 8], F32)
    if kind in ("attn0", "fused"):
        c.sbuf("rpm", [128, 128], BF16)
        c.sbuf("cs", [128, 2 * T], BF16)
    if kind == "fused":
        c.b["FF0"] = c.b["cs"][:, 0:4096]
    if kind in ("ffn0", "fused"):
        if kind == "ffn0":
            c.sbuf("FF0", [128, 4096], BF16)
        c.sbuf("GS", [32, T], BF16)
        c.sbuf("QZ", [32, 256], I32)
        c.sbuf("SZ", [32, 512], BF16)
        c.b["HS"] = c.b["lnb"][0:86, 45232:45744]
        c.b["SD"] = c.b["lnb"][0:86, 45744:49840]
        c.b["ZSD"] = c.b["lnb"][0:86, 49840:53936]
        c.b["QZD"] = c.b["FF0"].bitcast(I32)[0:86, 0:512]
        c.b["MB"] = c.b["FF0"][:, 256:384]
        c.b["MB2"] = c.b["lnb"][:, 45056:45232]
    if kind in ("ffn1", "fused"):
        c.sbuf("IND", [8, 128], BF16)
        c.sbuf("BB", [128, 32], F32)
        if kind == "ffn1":
            c.sbuf("identf", [128, 128], F32)
        else:
            c.b["identf"] = c.b["BIG0"].bitcast(F32)[:, 3392:3520]

    @c.sec("gpsimd")
    def _(g):
        def step(inst):
            inst.then_inc(c.i("gp"), 1)
            g.wait_ge(c.s("gp"), c.n["gp"])
        g.memset(c.b["ones_bf"][:1, :], 1.0)
        g.memset(c.b["ones_cb"][:, 0:1], 1.0)
        g.memset(c.b["ones_f"][:1, :], 1.0)
        g.memset(c.b["eps_t"][:1, :], EPS)
        last = g.memset(c.b["sc1"][:, :], 0.0)
        if kind in ("attn0", "fused"):
            rpm, scr = c.b["rpm"], c.b["BIG0"]
            step(g.memset(rpm[:, :], 1.0))
            step(g.memset(rpm[64:128, :], -1.0))
            step(g.affine_select(scr[:, 0:128], rpm[:, :],
                                 pattern=[[-1, 128]],
                                 compare_op=ALU.is_equal, fill=0.0,
                                 base=-64, channel_multiplier=1))
            step(g.affine_select(rpm[:, :], rpm[:, :], pattern=[[-1, 128]],
                                 compare_op=ALU.is_equal, fill=0.0,
                                 base=64, channel_multiplier=1))
            last = g.tensor_tensor(rpm[:, :], rpm[:, :], scr[:, 0:128],
                                   op=ALU.add)
        if kind in ("ffn1", "fused"):
            if kind == "ffn1":
                step(g.memset(c.b["identf"][:, :], 1.0))
                step(g.affine_select(c.b["identf"][:, :],
                                     c.b["identf"][:, :],
                                     pattern=[[-1, 128]],
                                     compare_op=ALU.is_equal, fill=0.0,
                                     base=0, channel_multiplier=1))
            step(g.memset(c.b["IND"][:8, :], 2.0))
            step(g.affine_select(c.b["IND"][:8, :], c.b["IND"][:8, :],
                                 pattern=[[1, 128]], compare_op=ALU.is_ge,
                                 fill=0.0, base=0, channel_multiplier=-16))
            last = g.affine_select(c.b["IND"][:8, :], c.b["IND"][:8, :],
                                   pattern=[[-1, 128]],
                                   compare_op=ALU.is_ge, fill=0.0,
                                   base=15, channel_multiplier=16)
        last.then_inc(c.i("gp"), 1)


def emit_rope_tables(c):
    r0 = c.b["rwb"]
    r1 = c.b["rb1"]
    posr = c.b["rwa"]
    pA = c.b["pA"]
    tmpf = c.b["BIG0"].bitcast(F32)
    gp0, dv0, ac0, pe0 = c.n["gp"], c.n["dv"], c.n["ac"], c.n["pe"]

    @c.sec("gpsimd")
    def _(g):
        g.wait_ge(c.s("gp"), gp0)
        g.iota(r0[0:1, 0:128], pattern=[[1, 128]], base=0,
               channel_multiplier=0, allow_small_or_imprecise_dtypes=True)
        g.iota(posr, pattern=[[1, T]], base=0, channel_multiplier=0,
               allow_small_or_imprecise_dtypes=True).then_inc(c.i("gp"), 1)

    @c.sec("vector")
    def _(v):
        v.wait_ge(c.s("gp"), gp0 + 1)
        v.tensor_scalar(r1[0:1, 0:128], r0[0:1, 0:128], -64.0, None,
                        op0=ALU.add).then_inc(c.i("dv"), 1)
        v.wait_ge(c.s("dv"), c.n["dv"])
        v.tensor_scalar(r1[0:1, 0:128], r1[0:1, 0:128], 0.0, None,
                        op0=ALU.is_ge).then_inc(c.i("dv"), 1)
        v.wait_ge(c.s("dv"), c.n["dv"])
        v.scalar_tensor_tensor(r0[0:1, 0:128], r1[0:1, 0:128], -64.0,
                               r0[0:1, 0:128], op0=ALU.mult,
                               op1=ALU.add).then_inc(c.i("dv"), 1)

    @c.sec("scalar")
    def _(s):
        s.wait_ge(c.s("dv"), dv0 + 3)
        s.activation(r0[0:1, 0:128], r0[0:1, 0:128], AF.Exp, bias=0.0,
                     scale=float(-np.log(10000.0) / 64.0)
                     ).then_inc(c.i("ac"), 1)

    @c.sec("tensor")
    def _(t):
        t.wait_ge(c.s("ac"), ac0 + 1)
        mm_tok(t, pA, r0[0:1, 0:128], posr, True,
               True).then_inc(c.i("pe"), 1)

    for which in ("cos", "sin"):
        shift = float(np.pi / 2) if which == "cos" else 0.0
        off = 0 if which == "cos" else T
        dvb = c.n["dv"]
        acb = c.n["ac"]

        @c.sec("vector")
        def _(v, shift=shift, acb=acb):
            v.wait_ge(c.s("pe"), pe0 + 1)
            if which == "sin":
                v.wait_ge(c.s("ac"), acb)
            v.tensor_scalar(tmpf[:, 0:T], pA[:, :], shift,
                            float(1.0 / TWO_PI), op0=ALU.add, op1=ALU.mult)
            v.tensor_scalar(tmpf[:, 0:T], tmpf[:, 0:T], float(2.0 ** 23),
                            float(2.0 ** 23), op0=ALU.add, op1=ALU.subtract)
            v.scalar_tensor_tensor(tmpf[:, 0:T], tmpf[:, 0:T], -TWO_PI,
                                   pA[:, :], op0=ALU.mult, op1=ALU.add)
            if shift:
                v.tensor_scalar(tmpf[:, 0:T], tmpf[:, 0:T], shift, None,
                                op0=ALU.add)
            v.tensor_scalar(tmpf[:, 0:T], tmpf[:, 0:T], -PI, PI,
                            op0=ALU.max, op1=ALU.min).then_inc(c.i("dv"), 1)

        @c.sec("scalar")
        def _(s, off=off, dvb=dvb):
            s.wait_ge(c.s("dv"), dvb + 1)
            s.activation(c.b["cs"][:, off:off + T], tmpf[:, 0:T], AF.Sin,
                         bias=0.0, scale=1.0).then_inc(c.i("ac"), 1)


def emit_ln(c, x_dram):
    pA, pB = c.b["pA"], c.b["pB"]
    b0f = c.b["BIG0"].bitcast(F32)
    xs = [b0f[:, 0:T], b0f[:, T:2 * T]]
    tmpf = b0f[:, 2 * T:3 * T]
    xb = c.b["BIG1"][:, 0:T]
    xq = c.b["BIG1"][:, T:2 * T]
    lnb = c.b["lnb"]
    xv = x_dram.rearrange("(cc p) t -> cc p t", p=128)
    ld0, dv0, ac0, pe0 = c.n["ld"], c.n["dv"], c.n["ac"], c.n["pe"]

    @c.sec("sync")
    def _(q):
        q.wait_ge(c.s("dv"), dv0)
        q.wait_ge(c.s("ac"), ac0)
        for cc in range(DC):
            if cc >= 2:
                q.wait_ge(c.s("dv"), dv0 + cc - 1)
                q.wait_ge(c.s("ac"), ac0 + cc - 1)
            q.dma_start(out=xs[cc % 2][:, :],
                        in_=xv[cc, :, :]).then_inc(c.i("ld", 16), 16)

    @c.sec("vector")
    def _(v):
        for cc in range(DC):
            v.wait_ge(c.s("ld"), ld0 + 16 * (cc + 1))
            if cc >= 1:
                v.wait_ge(c.s("pe"), pe0 + cc)
            v.tensor_copy(xb, xs[cc % 2][:, :]).then_inc(c.i("dv"), 1)

    @c.sec("scalar")
    def _(s):
        for cc in range(DC):
            s.wait_ge(c.s("ld"), ld0 + 16 * (cc + 1))
            if cc >= 1:
                s.wait_ge(c.s("pe"), pe0 + cc)
            s.square(xq, xs[cc % 2][:, :]).then_inc(c.i("ac"), 1)

    @c.sec("tensor")
    def _(t):
        for cc in range(DC):
            t.wait_ge(c.s("dv"), dv0 + cc + 1)
            t.wait_ge(c.s("ac"), ac0 + cc + 1)
            st, sp = (cc == 0), (cc == DC - 1)
            mm_tok(t, pA[0:1, :], c.b["ones_cb"][:, 0:1], xb, st, sp)
            last = None
            for jb in range(0, T, 512):
                last = t.matmul(pB[0:1, jb:jb + 512],
                                c.b["ones_cb"][:, 0:1],
                                xq[:, jb:jb + 512], start=st, stop=sp)
            last.then_inc(c.i("pe"), 1)

    dv1, ac1, pe1 = c.n["dv"], c.n["ac"], c.n["pe"]

    rm, rr, ra = c.b["rb1"], c.b["rwb"], c.b["rwa"]

    @c.sec("vector")
    def _(v):
        v.wait_ge(c.s("pe"), pe1)
        v.tensor_copy(rm, pA[0:1, :])
        v.tensor_copy(rr[0:1, 0:T], pB[0:1, :])
        v.tensor_scalar(rm, rm, float(1.0 / D), None, op0=ALU.mult)
        v.tensor_scalar(rr[0:1, 0:T], rr[0:1, 0:T], float(1.0 / D),
                        None, op0=ALU.mult)
        v.tensor_tensor(ra, rm, rm, op=ALU.mult)
        v.tensor_tensor(rr[0:1, 0:T], rr[0:1, 0:T], ra,
                        op=ALU.subtract).then_inc(c.i("dv"), 1)

    @c.sec("scalar")
    def _(s):
        s.wait_ge(c.s("dv"), dv1 + 1)
        s.activation(rr[0:1, 0:T], rr[0:1, 0:T], AF.Ln,
                     bias=c.b["eps_t"][:1, 0:1], scale=1.0)
        s.activation(rr[0:1, 0:T], rr[0:1, 0:T], AF.Exp, bias=0.0,
                     scale=-0.5).then_inc(c.i("ac"), 1)

    @c.sec("vector")
    def _(v):
        v.wait_ge(c.s("ac"), ac1 + 1)
        v.tensor_tensor(ra, rm, rr[0:1, 0:T], op=ALU.mult)
        v.tensor_scalar(ra, ra, -1.0, None, op0=ALU.mult).then_inc(
            c.i("dv"), 1)

    dv2 = c.n["dv"]

    @c.sec("tensor")
    def _(t):
        t.wait_ge(c.s("dv"), dv2)
        mm_tok(t, pA, c.b["ones_f"][:1, :], rr[0:1, 0:T], True, True)
        mm_tok(t, pB, c.b["ones_f"][:1, :], ra, True,
               True).then_inc(c.i("pe"), 1)

    pe2, ld1 = c.n["pe"], c.n["ld"]

    @c.sec("sync")
    def _(q):
        q.wait_ge(c.s("dv"), dv2)
        q.wait_ge(c.s("ac"), ac1 + 1)
        for cc in range(DC):
            if cc >= 2:
                q.wait_ge(c.s("dv"), dv2 + 2 * (cc - 1))
            q.dma_start(out=xs[cc % 2][:, :],
                        in_=xv[cc, :, :]).then_inc(c.i("ld", 16), 16)

    @c.sec("vector")
    def _(v):
        v.wait_ge(c.s("pe"), pe2)
        for cc in range(DC):
            v.wait_ge(c.s("ld"), ld1 + 16 * (cc + 1))
            v.tensor_tensor(tmpf[:, :], xs[cc % 2][:, :], pA[:, :],
                            op=ALU.mult).then_inc(c.i("dv"), 1)
            v.tensor_tensor(lnb[:, cc * T:(cc + 1) * T], tmpf[:, :],
                            pB[:, :], op=ALU.add).then_inc(c.i("dv"), 1)


def emit_norm_max(c, src_bf, sqr, dst_slot):
    pB = c.b["pB"]
    ac0, pe0, dv0 = c.n["ac"], c.n["pe"], c.n["dv"]

    @c.sec("scalar")
    def _(s):
        s.wait_ge(c.s("pe"), pe0)
        s.wait_ge(c.s("dv"), dv0)
        s.square(sqr, src_bf).then_inc(c.i("ac"), 1)

    @c.sec("tensor")
    def _(t):
        t.wait_ge(c.s("ac"), ac0 + 1)
        t.wait_ge(c.s("dv"), dv0)
        last = None
        for jb in range(0, T, 512):
            last = t.matmul(pB[0:1, jb:jb + 512], c.b["ones_cb"][:, 0:1],
                            sqr[:, jb:jb + 512], start=True, stop=True)
        last.then_inc(c.i("pe"), 1)

    @c.sec("vector")
    def _(v):
        v.wait_ge(c.s("pe"), pe0 + 1)
        v.reduce_max(dst_slot, pB[0:1, :],
                     axis=AX.X).then_inc(c.i("dv"), 1)
        v.wait_ge(c.s("dv"), c.n["dv"])


def emit_bias_from_norms(c, qslot, kslot):
    sc = c.b["sc1"]
    dv0, ac0 = c.n["dv"], c.n["ac"]

    @c.sec("vector")
    def _(v):
        v.wait_ge(c.s("dv"), dv0)
        v.tensor_tensor(sc[0:1, 2:3], qslot, kslot,
                        op=ALU.mult).then_inc(c.i("dv"), 1)

    @c.sec("scalar")
    def _(s):
        s.wait_ge(c.s("dv"), dv0 + 1)
        s.activation(sc[0:1, 2:3], sc[0:1, 2:3], AF.Ln, bias=0.0, scale=1.0)
        s.activation(sc[0:1, 2:3], sc[0:1, 2:3], AF.Exp, bias=0.0,
                     scale=0.5).then_inc(c.i("ac"), 1)

    @c.sec("vector")
    def _(v):
        v.wait_ge(c.s("ac"), ac0 + 1)
        v.tensor_scalar(sc[0:1, 2:3], sc[0:1, 2:3], -ISCALE, None,
                        op0=ALU.mult).then_inc(c.i("dv"), 1)

    dv1, pe1 = c.n["dv"], c.n["pe"]

    @c.sec("tensor")
    def _(t):
        t.wait_ge(c.s("dv"), dv1)
        t.matmul(c.b["pB"][:, 3:4], c.b["ones_f"][:1, :], sc[0:1, 2:3],
                 start=True, stop=True).then_inc(c.i("pe"), 1)

    @c.sec("vector")
    def _(v):
        v.wait_ge(c.s("pe"), pe1 + 1)
        v.tensor_copy(c.b["bias_t"][:, 0:1],
                      c.b["pB"][:, 3:4]).then_inc(c.i("dv"), 1)


def emit_attn_core(c, q_sb, k_sb, vg, vg_col, at_head_rows, causal):
    pA, pB = c.b["pA"], c.b["pB"]
    WA = c.b["WA"]
    eT = [WA[:, 0:1024], WA[:, 1024:2048]]
    atb = WA[:, 2048:3072]
    rb_b = WA[:, 3072:4096]
    r2 = c.b["rwb"]
    bias_dv = c.n["dv"]

    for ih in range(2):
        i0 = ih * 1024
        pS = [pA[:, 0:1024], pA[:, 1024:2048]]
        jts = [jt for jt in range(16)
               if (not causal) or (128 * jt < i0 + 1024)]

        def blocks_for(jt):
            if not causal:
                return [0, 1]
            return [ib for ib in range(2) if i0 + (ib + 1) * 512 > 128 * jt]

        first_jt, last_jt = {}, {}
        for jt in jts:
            for ib in blocks_for(jt):
                first_jt.setdefault(ib, jt)
                last_jt[ib] = jt

        def diag(jt):
            return causal and (128 * jt + 127 >= i0)

        pe0, ac0, gp0 = c.n["pe"], c.n["ac"], c.n["gp"]
        nj = len(jts)

        def avsums(t, jt, idx):
            last = None
            for ib in blocks_for(jt):
                st, sp = (first_jt[ib] == jt), (last_jt[ib] == jt)
                t.matmul(pB[:, ib * 512:(ib + 1) * 512],
                         vg[:, vg_col(jt):vg_col(jt) + 128],
                         eT[idx % 2][:, ib * 512:(ib + 1) * 512],
                         start=st, stop=sp)
                last = t.matmul(
                    pB[0:1, 1024 + ib * 512:1024 + (ib + 1) * 512],
                    c.b["ones_cb"][:, 0:1],
                    eT[idx % 2][:, ib * 512:(ib + 1) * 512],
                    start=st, stop=sp)
            return last

        dv_entry = c.n["dv"]

        @c.sec("tensor")
        def _(t):
            t.wait_ge(c.s("dv"), dv_entry)
            for idx, jt in enumerate(jts):
                if idx >= 2:
                    t.wait_ge(c.s("ac"), ac0 + idx - 1)
                last = None
                for ib in blocks_for(jt):
                    last = t.matmul(
                        pS[idx % 2][:, ib * 512:(ib + 1) * 512],
                        k_sb[:, jt * 128:(jt + 1) * 128],
                        q_sb[:, i0 + ib * 512:i0 + (ib + 1) * 512],
                        start=True, stop=True)
                last.then_inc(c.i("pe"), 1)
                if idx >= 1:
                    pj = jts[idx - 1]
                    if diag(pj):
                        t.wait_ge(c.s("gp"), gp0 + idx)
                    else:
                        t.wait_ge(c.s("ac"), ac0 + idx)
                    avsums(t, pj, idx - 1)
            pj = jts[nj - 1]
            if diag(pj):
                t.wait_ge(c.s("gp"), gp0 + nj)
            else:
                t.wait_ge(c.s("ac"), ac0 + nj)
            avsums(t, pj, nj - 1).then_inc(c.i("pe"), 1)

        @c.sec("scalar")
        def _(s):
            s.wait_ge(c.s("dv"), bias_dv)
            for idx, jt in enumerate(jts):
                s.wait_ge(c.s("pe"), pe0 + idx + 1)
                last = None
                for ib in blocks_for(jt):
                    last = s.activation(
                        eT[idx % 2][:, ib * 512:(ib + 1) * 512],
                        pS[idx % 2][:, ib * 512:(ib + 1) * 512],
                        AF.Exp, bias=c.b["bias_t"][:, 0:1], scale=ISCALE)
                last.then_inc(c.i("ac"), 1)

        if causal:
            @c.sec("gpsimd")
            def _(g):
                for idx, jt in enumerate(jts):
                    g.wait_ge(c.s("ac"), ac0 + idx + 1)
                    if diag(jt):
                        last = None
                        for ib in blocks_for(jt):
                            last = g.affine_select(
                                eT[idx % 2][:, ib * 512:(ib + 1) * 512],
                                eT[idx % 2][:, ib * 512:(ib + 1) * 512],
                                pattern=[[1, 512]], compare_op=ALU.is_ge,
                                fill=0.0, base=i0 + ib * 512 - 128 * jt,
                                channel_multiplier=-1)
                        last.then_inc(c.i("gp"), 1)
                    else:
                        g.engine_nop().then_inc(c.i("gp"), 1)

        pe_core, dvb = c.n["pe"], c.n["dv"]

        @c.sec("vector")
        def _(v):
            v.wait_ge(c.s("pe"), pe_core)
            v.tensor_copy(r2[0:1, 0:1024], pB[0:1, 1024:2048])
            v.reciprocal(r2[0:1, 0:1024],
                         r2[0:1, 0:1024]).then_inc(c.i("dv"), 1)

        @c.sec("tensor")
        def _(t):
            t.wait_ge(c.s("dv"), dvb + 1)
            last = None
            for jb in range(2):
                last = t.matmul(pA[:, jb * 512:(jb + 1) * 512],
                                c.b["ones_f"][:1, :],
                                r2[0:1, jb * 512:(jb + 1) * 512],
                                start=True, stop=True)
            last.then_inc(c.i("pe"), 1)

        dvc = c.n["dv"]

        @c.sec("vector")
        def _(v):
            v.wait_ge(c.s("pe"), pe_core + 1)
            v.tensor_copy(rb_b, pA[:, 0:1024]).then_inc(c.i("dv"), 1)
            v.tensor_tensor(atb, pB[:, 0:1024], rb_b,
                            op=ALU.mult).then_inc(c.i("dv"), 1)

        @c.sec("sync")
        def _(q):
            q.wait_ge(c.s("dv"), dvc + 2)
            q.dma_start(out=at_head_rows[:, i0:i0 + 1024],
                        in_=atb).then_inc(c.i("ld", 16), 16)


def _l0_qk(c, w_d, rowstride, col0, nrows, lnsrc, nch, dst_sb, wb2,
           lat=None):
    pA = c.b["pA"]
    pre = c.b["WA"][:, 0:2048]
    cs = c.b["cs"]
    rpm = c.b["rpm"]
    dv0 = c.n["dv"]

    @c.sec("gpsimd")
    def _(g):
        g.wait_ge(c.s("dv"), dv0)
        g.dma_start(out=wb2[:, 0:(nrows // 128) * 128],
                    in_=wslice_ap(w_d, rowstride, col0, 128, nrows)
                    ).then_inc(c.i("lg", 16), 16)

    pe0, lg0 = c.n["pe"], c.n["lg"]

    @c.sec("tensor")
    def _(t):
        t.wait_ge(c.s("lg"), lg0)
        t.wait_ge(c.s("dv"), dv0)
        last = None
        for ch in range(nch):
            if lat is None:
                last = mm_tok(t, pA, wb2[:, ch * 128:(ch + 1) * 128],
                              lnsrc, ch == 0, ch == nch - 1, coff=ch * T)
            else:
                last = mm_tok(t, pA, wb2[:, ch * 128:(ch + 1) * 128],
                              lat[ch], ch == 0, ch == nch - 1)
        last.then_inc(c.i("pe"), 1)

    ac0 = c.n["ac"]

    @c.sec("scalar")
    def _(s):
        s.wait_ge(c.s("pe"), pe0 + 1)
        s.copy(pre, pA[:, :]).then_inc(c.i("ac"), 1)

    @c.sec("tensor")
    def _(t):
        t.wait_ge(c.s("ac"), ac0 + 1)
        mm_tok(t, pA, rpm[:, :], pre, True, True).then_inc(c.i("pe"), 1)

    @c.sec("vector")
    def _(v):
        v.wait_ge(c.s("pe"), pe0 + 2)
        v.tensor_tensor(dst_sb[:, :], pre, cs[:, 0:T], op=ALU.mult)
        v.tensor_tensor(pre, pA[:, :], cs[:, T:2 * T], op=ALU.mult)
        v.tensor_tensor(dst_sb[:, :], dst_sb[:, :], pre,
                        op=ALU.add).then_inc(c.i("dv"), 1)


ATTN_GROUPS = 8
DBG_T2 = False
DBG_LC = LC


def emit_attn_l0(c, wq_d, wc_d, wk_d, wv_d, at_scr):
    pA = c.b["pA"]
    lnb = c.b["lnb"]
    BIG0, BIG1, WB = c.b["BIG0"], c.b["BIG1"], c.b["WB"]
    lat = [BIG0[:, i * T:(i + 1) * T] for i in range(4)]
    q_sb = BIG0[:, 8192:10240]
    k_sb = BIG0[:, 10240:12288]
    vg = BIG1[:, 0:8192]
    wb2 = BIG1[:, 8192:12288]
    sqr = c.b["WA"][:, 0:2048]
    sc = c.b["sc1"]
    atv = at_scr.rearrange("(h p) t -> h p t", p=128)

    for lc2 in range(DBG_LC):
        wdst = WB if lc2 % 2 == 0 else wb2
        dv_pre = c.n["dv"]

        @c.sec("gpsimd")
        def _(g, wdst=wdst, lc2=lc2):
            g.wait_ge(c.s("dv"), dv_pre)
            if lc2 >= 2:
                g.wait_ge(c.s("pe"), c.n["pe"] - 1)
            g.dma_start(out=wdst[:, 0:4096],
                        in_=wslice_ap(wc_d, LAT, lc2 * 128, 128, D)
                        ).then_inc(c.i("lg", 16), 16)

        lgL = c.n["lg"]

        @c.sec("tensor")
        def _(t, wdst=wdst):
            t.wait_ge(c.s("lg"), lgL)
            t.wait_ge(c.s("dv"), dv_pre)
            last = None
            for ch in range(DC):
                last = mm_tok(t, pA, wdst[:, ch * 128:(ch + 1) * 128],
                              lnb, ch == 0, ch == DC - 1, coff=ch * T)
            last.then_inc(c.i("pe"), 1)

        peL = c.n["pe"]

        @c.sec("vector")
        def _(v, lc2=lc2):
            v.wait_ge(c.s("pe"), peL)
            v.tensor_copy(lat[lc2], pA[:, :]).then_inc(c.i("dv"), 1)

    for grp in range(ATTN_GROUPS):
        dvv0 = c.n["dv"]

        @c.sec("gpsimd")
        def _(g):
            g.wait_ge(c.s("dv"), dvv0)
            g.dma_start(out=WB[:, 0:2048],
                        in_=wslice_ap(wv_d, D, grp * 512, 512, LAT)
                        ).then_inc(c.i("lg", 16), 16)

        pev, lgv = c.n["pe"], c.n["lg"]

        @c.sec("tensor")
        def _(t):
            t.wait_ge(c.s("lg"), lgv)
            t.wait_ge(c.s("dv"), dvv0)
            for tt in range(16):
                if tt >= 4 and tt % 4 == 0:
                    t.wait_ge(c.s("dv"), dvv0 + tt // 4)
                last = None
                for lc2 in range(LC):
                    last = t.matmul(
                        pA[:, (tt % 4) * 512:(tt % 4) * 512 + 512],
                        lat[lc2][:, tt * 128:(tt + 1) * 128],
                        WB[:, lc2 * 512:(lc2 + 1) * 512],
                        start=(lc2 == 0), stop=(lc2 == LC - 1))
                if tt % 4 == 3:
                    last.then_inc(c.i("pe"), 1)

        @c.sec("vector")
        def _(v):
            for qt in range(4):
                v.wait_ge(c.s("pe"), pev + qt + 1)
                v.tensor_copy(vg[:, qt * 2048:(qt + 1) * 2048],
                              pA[:, :]).then_inc(c.i("dv"), 1)

        for hg in range(4):
            h = grp * 4 + hg
            _l0_qk(c, wq_d, D, h * 128, D, lnb, DC, q_sb, wb2)
            _l0_qk(c, wk_d, D, h * 128, LAT, None, LC, k_sb, wb2, lat=lat)
            emit_norm_max(c, q_sb, sqr, sc[0:1, 0:1])
            emit_norm_max(c, k_sb, sqr, sc[0:1, 1:2])
            emit_bias_from_norms(c, sc[0:1, 0:1], sc[0:1, 1:2])
            emit_attn_core(c, q_sb, k_sb, vg,
                           (lambda jt, hg=hg: jt * 512 + hg * 128),
                           atv[h, :, :], causal=False)


def emit_wo_residual(c, wo_d, at_scr, x_in, x_out):
    pA, pB = c.b["pA"], c.b["pB"]
    lnb = c.b["lnb"]
    WB, BIG0, BIG1 = c.b["WB"], c.b["BIG0"], c.b["BIG1"]
    b0f = BIG0.bitcast(F32)
    b1f = BIG1.bitcast(F32)
    xs = [b0f[:, 0:T], b0f[:, T:2 * T]]
    outf = [b1f[:, 0:T], b1f[:, T:2 * T]]
    wb2 = BIG1[:, 8192:12288]
    atv = at_scr.rearrange("(h p) t -> h p t", p=128)
    xiv = x_in.rearrange("(cc p) t -> cc p t", p=128)
    xov = x_out.rearrange("(cc p) t -> cc p t", p=128)
    dv0, ld0 = c.n["dv"], c.n["ld"]

    @c.sec("sync")
    def _(q):
        q.wait_ge(c.s("dv"), dv0)
        for hh in range(H):
            q.dma_start(out=lnb[:, hh * T:(hh + 1) * T],
                        in_=atv[hh, :, :]).then_inc(c.i("ld", 16), 16)

    st_base = {}
    for dm in range(DC):
        wdst = WB if dm % 2 == 0 else wb2
        pdst = pA if dm % 2 == 0 else pB

        @c.sec("gpsimd")
        def _(g, wdst=wdst, dm=dm):
            g.wait_ge(c.s("dv"), dv0)
            if dm >= 2:
                g.wait_ge(c.s("pe"), c.n["pe"] - 1)
            g.dma_start(out=wdst[:, 0:4096],
                        in_=wslice_ap(wo_d, D, dm * 128, 128, D)
                        ).then_inc(c.i("lg", 16), 16)

        ldx = c.n["ld"]

        @c.sec("sync")
        def _(q, dm=dm):
            if dm >= 2:
                q.wait_ge(c.s("dv"), dv0 + dm - 1)
            q.dma_start(out=xs[dm % 2][:, :],
                        in_=xiv[dm, :, :]).then_inc(c.i("ld", 16), 16)

        lgw, pew = c.n["lg"], c.n["pe"]

        @c.sec("tensor")
        def _(t, wdst=wdst, pdst=pdst, dm=dm):
            t.wait_ge(c.s("lg"), lgw)
            if dm == 0:
                t.wait_ge(c.s("ld"), ld0 + 16 * H)
            if dm >= 2:
                t.wait_ge(c.s("dv"), dv0 + dm - 1)
            last = None
            for ch in range(H):
                last = mm_tok(t, pdst, wdst[:, ch * 128:(ch + 1) * 128],
                              lnb, ch == 0, ch == H - 1, coff=ch * T)
            last.then_inc(c.i("pe"), 1)

        @c.sec("vector")
        def _(v, pdst=pdst, dm=dm):
            v.wait_ge(c.s("pe"), pew + 1)
            v.wait_ge(c.s("ld"), ldx + 16)
            if dm >= 2:
                v.wait_ge(c.s("ls"), st_base[dm - 2] + 16)
            v.tensor_tensor(outf[dm % 2], xs[dm % 2][:, :], pdst[:, :],
                            op=ALU.add).then_inc(c.i("dv"), 1)

        dvw = c.n["dv"]
        st_base[dm] = c.n["ls"]

        @c.sec("scalar")
        def _(s, dm=dm):
            s.wait_ge(c.s("dv"), dvw)
            s.dma_start(out=xov[dm, :, :],
                        in_=outf[dm % 2]).then_inc(c.i("ls", 16), 16)


def emit_attn_l1(c, wq_d, wk_d, wv_d, at_scr):
    pA = c.b["pA"]
    lnb = c.b["lnb"]
    BIG0, BIG1, WB = c.b["BIG0"], c.b["BIG1"], c.b["WB"]
    q_sb = BIG0[:, 8192:10240]
    k_sb = BIG0[:, 10240:12288]
    vg = BIG1[:, 0:2048]
    wb2 = BIG1[:, 8192:12288]
    sqr = c.b["WA"][:, 0:2048]
    sc = c.b["sc1"]
    atv = at_scr.rearrange("(h p) t -> h p t", p=128)
    dv0 = c.n["dv"]

    @c.sec("gpsimd")
    def _(g):
        g.wait_ge(c.s("dv"), dv0)
        g.dma_start(out=WB[:, 0:4096],
                    in_=wslice_ap(wk_d, HD, 0, HD, D)).then_inc(c.i("lg", 16), 16)
        g.dma_start(out=wb2[:, 0:4096],
                    in_=wslice_ap(wv_d, HD, 0, HD, D)).then_inc(c.i("lg", 16), 16)

    lg0, pe0 = c.n["lg"], c.n["pe"]

    @c.sec("tensor")
    def _(t):
        t.wait_ge(c.s("lg"), lg0 - 16)
        t.wait_ge(c.s("dv"), dv0)
        last = None
        for ch in range(DC):
            last = mm_tok(t, pA, WB[:, ch * 128:(ch + 1) * 128], lnb,
                          ch == 0, ch == DC - 1, coff=ch * T)
        last.then_inc(c.i("pe"), 1)

    @c.sec("vector")
    def _(v):
        v.wait_ge(c.s("pe"), pe0 + 1)
        v.tensor_copy(k_sb[:, :], pA[:, :]).then_inc(c.i("dv"), 1)

    pev, dvv = c.n["pe"], c.n["dv"]

    @c.sec("tensor")
    def _(t):
        t.wait_ge(c.s("lg"), lg0)
        t.wait_ge(c.s("dv"), dvv)
        for tt in range(16):
            if tt >= 4:
                t.wait_ge(c.s("dv"), dvv + tt - 3)
            last = None
            for ch in range(DC):
                last = t.matmul(
                    pA[:, (tt % 4) * 512:(tt % 4) * 512 + 128],
                    lnb[:, ch * T + tt * 128:ch * T + (tt + 1) * 128],
                    wb2[:, ch * 128:(ch + 1) * 128],
                    start=(ch == 0), stop=(ch == DC - 1))
            last.then_inc(c.i("pe"), 1)

    @c.sec("vector")
    def _(v):
        for tt in range(16):
            v.wait_ge(c.s("pe"), pev + tt + 1)
            v.tensor_copy(vg[:, tt * 128:(tt + 1) * 128],
                          pA[:, (tt % 4) * 512:(tt % 4) * 512 + 128]
                          ).then_inc(c.i("dv"), 1)

    emit_norm_max(c, k_sb, sqr, sc[0:1, 1:2])

    for h in range(4 * ATTN_GROUPS):
        dvq = c.n["dv"]

        @c.sec("gpsimd")
        def _(g):
            g.wait_ge(c.s("dv"), dvq)
            g.dma_start(out=wb2[:, 0:4096],
                        in_=wslice_ap(wq_d, D, h * 128, 128, D)
                        ).then_inc(c.i("lg", 16), 16)

        lgq, peq = c.n["lg"], c.n["pe"]

        @c.sec("tensor")
        def _(t):
            t.wait_ge(c.s("lg"), lgq)
            t.wait_ge(c.s("dv"), dvq)
            last = None
            for ch in range(DC):
                last = mm_tok(t, pA, wb2[:, ch * 128:(ch + 1) * 128],
                              lnb, ch == 0, ch == DC - 1, coff=ch * T)
            last.then_inc(c.i("pe"), 1)

        @c.sec("vector")
        def _(v):
            v.wait_ge(c.s("pe"), peq + 1)
            v.tensor_copy(q_sb[:, :], pA[:, :]).then_inc(c.i("dv"), 1)

        emit_norm_max(c, q_sb, sqr, sc[0:1, 0:1])
        emit_bias_from_norms(c, sc[0:1, 0:1], sc[0:1, 1:2])
        emit_attn_core(c, q_sb, k_sb, vg, (lambda jt: jt * 128),
                       atv[h, :, :], causal=True)


def emit_gs(c):
    pA = c.b["pA"]
    lnb = c.b["lnb"]
    mast = c.b["MB"]
    dv0 = c.n["dv"]

    @c.sec("gpsimd")
    def _(g):
        g.memset(mast[:, :], 1.0).then_inc(c.i("gp"), 1)
        g.wait_ge(c.s("gp"), c.n["gp"])
        g.affine_select(mast[:, :], mast[:, :], pattern=[[1, 128]],
                        compare_op=ALU.is_equal, fill=0.0, base=-31,
                        channel_multiplier=0).then_inc(c.i("gp"), 1)

    gp0 = c.n["gp"]

    @c.sec("tensor")
    def _(t):
        t.wait_ge(c.s("gp"), gp0)
        t.wait_ge(c.s("dv"), dv0)
        last = None
        for cc in range(DC):
            last = mm_tok(t, pA[0:32, :], mast[:, 31 - cc:63 - cc],
                          lnb, cc == 0, cc == DC - 1, coff=cc * T)
        last.then_inc(c.i("pe"), 1)

    pe0 = c.n["pe"]

    @c.sec("vector")
    def _(v):
        v.wait_ge(c.s("pe"), pe0)
        v.tensor_copy(c.b["GS"][:, :], pA[0:32, :]).then_inc(c.i("dv"), 1)


def emit_gateup(c, qwgu_d, qzgu_d, sgu_d, h_scr):
    pA, pB = c.b["pA"], c.b["pB"]
    lnb = c.b["lnb"]
    GS, QZ, SZ, FF0 = c.b["GS"], c.b["QZ"], c.b["SZ"], c.b["FF0"]
    WA, WB = c.b["WA"], c.b["WB"]
    BIG0, BIG1 = c.b["BIG0"], c.b["BIG1"]
    pkG = BIG0.bitcast(I32)
    pkU = BIG1.bitcast(I32)
    sbG = BIG0[:, 8192:12288]
    sbU = BIG1[:, 8192:12288]
    sg = FF0[:, 0:2048]
    hb = FF0[:, 2048:4096]
    gs_dv = c.n["dv"]

    for c_hi in range(11):
        w = 128 if c_hi < 10 else 96
        ld0 = c.n["ld"]

        @c.sec("sync")
        def _(q, c_hi=c_hi, w=w):
            if c_hi == 0:
                q.wait_ge(c.s("dv"), gs_dv)
            if c_hi >= 1:
                q.wait_ge(c.s("pe"), c.n["pe"])
            q.dma_start(out=pkG[:, 0:32 * w],
                        in_=wslice_ap(qwgu_d, 2752, c_hi * 128, w, D)
                        ).then_inc(c.i("ld", 16), 16)
            q.dma_start(out=pkU[:, 0:32 * w],
                        in_=wslice_ap(qwgu_d, 2752, 1376 + c_hi * 128, w, D)
                        ).then_inc(c.i("ld", 16), 16)
            q.dma_start(out=QZ[:, 0:w],
                        in_=bass.AP(qzgu_d.tensor, c_hi * 128,
                                    [[2752, 32], [1, 1], [1, w]])
                        ).then_inc(c.i("ld", 16), 16)
            q.dma_start(out=QZ[:, 128:128 + w],
                        in_=bass.AP(qzgu_d.tensor, 1376 + c_hi * 128,
                                    [[2752, 32], [1, 1], [1, w]])
                        ).then_inc(c.i("ld", 16), 16)

        for k in range(8):
            base = NHB[c_hi * 8 + k][0]
            lg0 = c.n["lg"]

            lg0 = c.n["lg"]
            dv_prev = c.n["dv"]

            @c.sec("gpsimd")
            def _(g, base=base, w=w, k=k, c_hi=c_hi):
                if c_hi == 0 and k == 0:
                    g.wait_ge(c.s("dv"), gs_dv)
                g.wait_ge(c.s("dv"), dv_prev)
                for side in range(2):
                    big = BIG0 if side == 0 else BIG1
                    soff = (0 if side == 0 else INTER) + base
                    for q in range(4):
                        if q >= 2:
                            g.wait_ge(c.s("dv"),
                                      dv_prev + 4 * side + q - 1)
                        g.dma_start(
                            out=big[:, 8192 + (q % 2) * 1024:
                                    8192 + (q % 2) * 1024 + 8 * w],
                            in_=bass.AP(sgu_d.tensor,
                                        soff + 8 * q * 22016,
                                        [[0, 128], [22016, 8], [1, w]])
                            ).then_inc(c.i("lg", 16), 16)
                g.dma_start(out=SZ[:, 0:w],
                            in_=bass.AP(sgu_d.tensor, base,
                                        [[22016, 32], [1, 1], [1, w]])
                            ).then_inc(c.i("lg", 16), 16)
                g.dma_start(out=SZ[:, 128:128 + w],
                            in_=bass.AP(sgu_d.tensor, INTER + base,
                                        [[22016, 32], [1, 1], [1, w]])
                            ).then_inc(c.i("lg", 16), 16)

            dvk = c.n["dv"]

            @c.sec("vector")
            def _(v, k=k, w=w, c_hi=c_hi):
                v.wait_ge(c.s("ld"), ld0 + 64)
                for side in range(2):
                    v.wait_ge(c.s("pe"), c.n["pe"] - 1 + side)
                    pk = pkG if side == 0 else pkU
                    wout = WA if side == 0 else WB
                    scr = (BIG0 if side == 0 else
                           BIG1).bitcast(I32)[:, 5120:6144]
                    for q in range(4):
                        v.wait_ge(c.s("lg"),
                                  lg0 + 16 * (4 * side + q + 1))
                        sq = (BIG0 if side == 0 else BIG1)[
                            :, 8192 + (q % 2) * 1024:
                            8192 + (q % 2) * 1024 + 8 * w]
                        v.tensor_scalar(scr[:, 0:8 * w],
                                        pk[:, 8 * q * w:(8 * q + 8) * w],
                                        4 * k, 15,
                                        op0=ALU.logical_shift_right,
                                        op1=ALU.bitwise_and)
                        v.tensor_tensor(wout[:, 8 * q * w:(8 * q + 8) * w],
                                        scr[:, 0:8 * w], sq,
                                        op=ALU.mult).then_inc(c.i("dv"), 1)
                v.wait_ge(c.s("lg"), lg0 + 160)
                zscr = BIG0.bitcast(I32)[0:32, 5120:5120 + 128]
                v.tensor_scalar(zscr[:, 0:w], QZ[:, 0:w], 4 * k, 15,
                                op0=ALU.logical_shift_right,
                                op1=ALU.bitwise_and)
                v.scalar_tensor_tensor(SZ[:, 256:256 + w], zscr[:, 0:w],
                                       -1.0, SZ[:, 0:w], op0=ALU.mult,
                                       op1=ALU.mult)
                v.tensor_scalar(zscr[:, 0:w], QZ[:, 128:128 + w],
                                4 * k, 15, op0=ALU.logical_shift_right,
                                op1=ALU.bitwise_and)
                v.scalar_tensor_tensor(SZ[:, 384:384 + w], zscr[:, 0:w],
                                       -1.0, SZ[:, 128:128 + w],
                                       op0=ALU.mult,
                                       op1=ALU.mult).then_inc(c.i("dv"), 1)

            peg = c.n["pe"]

            @c.sec("tensor")
            def _(t, w=w, ac_prev=c.n["ac"]):
                t.wait_ge(c.s("dv"), dvk + 4)
                t.wait_ge(c.s("ac"), ac_prev)
                t.wait_ge(c.s("dv"), gs_dv)
                for cc in range(DC):
                    mm_tok(t, pA[0:w, :] if w < 128 else pA,
                           WA[:, cc * w:(cc + 1) * w], lnb, cc == 0, False,
                           coff=cc * T)
                t.wait_ge(c.s("dv"), dvk + 9)
                mm_tok(t, pA[0:w, :] if w < 128 else pA,
                       SZ[0:32, 256:256 + w], GS[0:32, :], False,
                       True).then_inc(c.i("pe"), 1)
                for cc in range(DC):
                    mm_tok(t, pB[0:w, :] if w < 128 else pB,
                           WB[:, cc * w:(cc + 1) * w], lnb, cc == 0, False,
                           coff=cc * T)
                mm_tok(t, pB[0:w, :] if w < 128 else pB,
                       SZ[0:32, 384:384 + w], GS[0:32, :], False,
                       True).then_inc(c.i("pe"), 1)

            ach = c.n["ac"]

            @c.sec("scalar")
            def _(s, w=w):
                s.wait_ge(c.s("pe"), peg + 1)
                s.activation(sg[0:w if w < 128 else 128, :],
                             pA[0:w, :] if w < 128 else pA[:, :],
                             AF.Silu, bias=0.0, scale=1.0
                             ).then_inc(c.i("ac"), 1)

            dvh = c.n["dv"]

            @c.sec("vector")
            def _(v, w=w):
                v.wait_ge(c.s("ac"), ach + 1)
                v.wait_ge(c.s("pe"), peg + 2)
                v.tensor_tensor(hb[0:w if w < 128 else 128, :],
                                sg[0:w if w < 128 else 128, :],
                                pB[0:w, :] if w < 128 else pB[:, :],
                                op=ALU.mult).then_inc(c.i("dv"), 1)

            @c.sec("sync")
            def _(q, base=base, w=w):
                q.wait_ge(c.s("dv"), dvh + 1)
                q.dma_start(out=bass.AP(h_scr.tensor, base * T,
                                        [[T, w], [1, 1], [1, T]]),
                            in_=hb[0:w if w < 128 else 128, :]
                            ).then_inc(c.i("ld", 16), 16)


def emit_wdn_dequant(c, qwdn_d, sdn_d, wdn4):
    BIG0 = c.b["BIG0"]
    pk = BIG0.bitcast(I32)
    sb = BIG0[:, 8192:12288]
    WA, WB = c.b["WA"], c.b["WB"]
    pe_entry, dv_entry = c.n["pe"], c.n["dv"]
    for c_hi in range(11):
        w = 128 if c_hi < 10 else 96
        ng = 8 if c_hi < 10 else 6
        lg0 = c.n["lg"]

        @c.sec("gpsimd")
        def _(g, c_hi=c_hi, ng=ng):
            if c_hi == 0:
                g.wait_ge(c.s("pe"), pe_entry)
                g.wait_ge(c.s("dv"), dv_entry)
            if c_hi >= 1:
                g.wait_ge(c.s("dv"), c.n["dv"])
            for g2 in range(ng):
                g.dma_start(
                    out=sb[16 * g2:16 * (g2 + 1), 0:4096],
                    in_=bass.AP(sdn_d.tensor, (8 * c_hi + g2) * 4096,
                                [[0, 16], [1, 1], [1, 4096]])
                    ).then_inc(c.i("lg", 16), 16)

        for k in range(8):
            ch = c_hi * 8 + k
            base = NHB[ch][0]
            buf = pk[:, 0:512] if k % 2 == 0 else pk[:, 512:1024]
            ld0 = c.n["ld"]

            @c.sec("sync")
            def _(q, base=base, w=w, buf=buf, k=k, c_hi=c_hi):
                if c_hi == 0 and k == 0:
                    q.wait_ge(c.s("pe"), pe_entry)
                    q.wait_ge(c.s("dv"), dv_entry)
                if k < 2:
                    q.wait_ge(c.s("dv"), c.n["dv"])
                if k >= 2:
                    q.wait_ge(c.s("dv"), c.n["dv"] - 8)
                q.dma_start(out=buf[0:w, :],
                            in_=bass.AP(qwdn_d.tensor, base * 512,
                                        [[512, w], [1, 1], [1, 512]])
                            ).then_inc(c.i("ld", 16), 16)

            dv0 = c.n["dv"]
            ls0 = c.n["ls"]

            for k2 in range(8):
                @c.sec("vector")
                def _(v, w=w, buf=buf, k2=k2):
                    if k2 == 0:
                        v.wait_ge(c.s("ld"), ld0 + 16)
                        v.wait_ge(c.s("lg"), lg0 + 16 * ng)
                    if k2 < 2:
                        v.wait_ge(c.s("ls"), ls0)
                    if k2 >= 2:
                        v.wait_ge(c.s("ls"), ls0 + 16 * (k2 - 1))
                    dst = WA[:, 0:512] if k2 % 2 == 0 else WA[:, 512:1024]
                    wscr = WA.bitcast(I32)[:, 1024:1536]
                    v.tensor_scalar(wscr[0:w, :], buf[0:w, :], 4 * k2, 15,
                                    op0=ALU.logical_shift_right,
                                    op1=ALU.bitwise_and)
                    v.tensor_tensor(dst[0:w, :], wscr[0:w, :],
                                    sb[0:w, k2 * 512:(k2 + 1) * 512],
                                    op=ALU.mult).then_inc(c.i("dv"), 1)

                @c.sec("scalar")
                def _(s, w=w, k2=k2, ch=ch):
                    s.wait_ge(c.s("dv"), dv0 + k2 + 1)
                    src = WA[:, 0:512] if k2 % 2 == 0 else WA[:, 512:1024]
                    s.dma_start(
                        out=bass.AP(wdn4.tensor,
                                    (k2 * 4) * (128 * 11264) + ch * 128,
                                    [[11264, w], [128 * 11264, 4],
                                     [1, 128]]),
                        in_=src[0:w, :]).then_inc(c.i("ls", 16), 16)


def emit_down(c, x_in, x_out, h_scr, wdn4, qzdn_d, sdn_d):
    pA, pB = c.b["pA"], c.b["pB"]
    lnb = c.b["lnb"]
    BIG0, BIG1 = c.b["BIG0"], c.b["BIG1"]
    WA, WB = c.b["WA"], c.b["WB"]
    SD, ZSD, QZD, HS = c.b["SD"], c.b["ZSD"], c.b["QZD"], c.b["HS"]
    FF0 = c.b["FF0"]
    ff = FF0.bitcast(F32)
    mast2 = c.b["MB2"]
    dvp = c.n["dv"]

    @c.sec("gpsimd")
    def _(g):
        g.wait_ge(c.s("pe"), c.n["pe"])
        g.memset(mast2[:, :], 1.0).then_inc(c.i("gp"), 1)
        g.wait_ge(c.s("gp"), c.n["gp"])
        g.affine_select(mast2[:, :], mast2[:, :], pattern=[[16, 176]],
                        compare_op=ALU.is_ge, fill=0.0,
                        base=-16 * 80 + 15, channel_multiplier=-1
                        ).then_inc(c.i("gp"), 1)
        g.wait_ge(c.s("gp"), c.n["gp"])
        g.affine_select(mast2[:, :], mast2[:, :], pattern=[[-16, 176]],
                        compare_op=ALU.is_ge, fill=0.0,
                        base=16 * 80, channel_multiplier=1
                        ).then_inc(c.i("gp"), 1)

    mb2_gp = c.n["gp"]

    ld0 = c.n["ld"]

    @c.sec("sync")
    def _(q):
        q.wait_ge(c.s("dv"), dvp)
        q.dma_start(out=QZD[:, :],
                    in_=bass.AP(qzdn_d.tensor, 0,
                                [[512, 86], [1, 1], [1, 512]])
                    ).then_inc(c.i("ld", 16), 16)

    @c.sec("gpsimd")
    def _(g):
        g.wait_ge(c.s("dv"), dvp)
        g.dma_start(out=SD[:, :],
                    in_=bass.AP(sdn_d.tensor, 0,
                                [[4096, 86], [1, 1], [1, 4096]])
                    ).then_inc(c.i("lg", 16), 16)

    lgz = c.n["lg"]
    dvz0 = c.n["dv"]

    @c.sec("vector")
    def _(v):
        v.wait_ge(c.s("ld"), ld0 + 16)
        v.wait_ge(c.s("lg"), lgz)
        zdscr = c.b["FF0"].bitcast(I32)[0:86, 512:1024]
        last = None
        for k2 in range(8):
            v.tensor_scalar(zdscr[:, :], QZD[:, :],
                            4 * k2, 15, op0=ALU.logical_shift_right,
                            op1=ALU.bitwise_and)
            last = v.scalar_tensor_tensor(
                ZSD[:, k2 * 512:(k2 + 1) * 512], zdscr[:, :], -1.0,
                SD[:, k2 * 512:(k2 + 1) * 512], op0=ALU.mult,
                op1=ALU.mult)
        last.then_inc(c.i("dv"), 1)

    zs_dv = c.n["dv"]
    ls_deq = c.n["ls"]
    xiv = x_in
    xov = x_out

    for tt in range(4):
        ldh = c.n["ld"]

        @c.sec("sync")
        def _(q, tt=tt):
            q.wait_ge(c.s("pe"), c.n["pe"])
            q.dma_start(out=lnb[:, 0:80 * 512],
                        in_=bass.AP(h_scr.tensor, tt * 512,
                                    [[T, 128], [128 * T, 80], [1, 512]])
                        ).then_inc(c.i("ld", 16), 16)
            q.dma_start(out=lnb[0:96, 80 * 512:88 * 512],
                        in_=bass.AP(h_scr.tensor, 10240 * T + tt * 512,
                                    [[T, 96], [96 * T, 8], [1, 512]])
                        ).then_inc(c.i("ld", 16), 16)

        peh = c.n["pe"]

        @c.sec("tensor")
        def _(t, tt=tt):
            t.wait_ge(c.s("ld"), ldh + 32)
            t.wait_ge(c.s("dv"), zs_dv)
            t.wait_ge(c.s("gp"), mb2_gp)
            last = None
            for ch in range(88):
                w = NHB[ch][1]
                last = t.matmul(pA[0:86, 0:512],
                                mast2[0:w, 80 - 8 * NHB[ch][2]:
                                      80 - 8 * NHB[ch][2] + 86],
                                lnb[0:w, ch * 512:(ch + 1) * 512],
                                start=(ch == 0), stop=(ch == 87))
            last.then_inc(c.i("pe"), 1)

        dvh = c.n["dv"]

        @c.sec("vector")
        def _(v):
            v.wait_ge(c.s("pe"), peh + 1)
            v.tensor_copy(HS[:, :], pA[0:86, 0:512]).then_inc(c.i("dv"), 1)

        for ot in range(32):
            k2, c2hi = ot // 4, ot % 4
            wbuf = BIG0 if ot % 2 == 0 else BIG1
            lgw = c.n["lg"]

            @c.sec("gpsimd")
            def _(g, ot=ot, wbuf=wbuf, tt=tt, ls_deq=ls_deq):
                if tt == 0 and ot < 2:
                    g.wait_ge(c.s("dv"), zs_dv)
                    g.wait_ge(c.s("ls"), ls_deq)
                if ot >= 2 or tt > 0:
                    g.wait_ge(c.s("pe"), c.n["pe"] - 1)
                g.dma_start(out=wbuf[:, 0:11264],
                            in_=bass.AP(wdn4.tensor, ot * (128 * 11264),
                                        [[11264, 128], [1, 1], [1, 11264]])
                            ).then_inc(c.i("lg", 16), 16)

            ldx = c.n["ld"]
            row0 = 1024 * c2hi + k2

            @c.sec("sync")
            def _(q, row0=row0, tt=tt, ot=ot):
                q.wait_ge(c.s("dv"), zs_dv)
                if ot >= 2 or tt > 0:
                    q.wait_ge(c.s("dv"), c.n["dv"] - 1)
                q.dma_start(out=ff[:, 0:512] if ot % 2 == 0
                            else ff[:, 512:1024],
                            in_=bass.AP(xiv.tensor, row0 * T + tt * 512,
                                        [[8 * T, 128], [1, 1], [1, 512]])
                            ).then_inc(c.i("ld", 16), 16)

            ped = c.n["pe"]

            @c.sec("tensor")
            def _(t, ot=ot, wbuf=wbuf):
                t.wait_ge(c.s("lg"), lgw + 16)
                t.wait_ge(c.s("dv"), dvh + 1)
                if ot >= 2 or tt > 0:
                    t.wait_ge(c.s("dv"), c.n["dv"] - 1)
                ps = pA[:, 512:1024] if ot % 2 == 0 else pA[:, 1024:1536]
                for ch in range(88):
                    w = NHB[ch][1]
                    t.matmul(ps, wbuf[0:w, ch * 128:(ch + 1) * 128],
                             lnb[0:w, ch * 512:(ch + 1) * 512],
                             start=(ch == 0), stop=False)
                t.matmul(ps, ZSD[:, k2 * 512 + c2hi * 128:
                                 k2 * 512 + c2hi * 128 + 128],
                         HS[:, :], start=False,
                         stop=True).then_inc(c.i("pe"), 1)

            dvd = c.n["dv"]

            @c.sec("vector")
            def _(v, ot=ot):
                v.wait_ge(c.s("pe"), ped + 1)
                v.wait_ge(c.s("ld"), ldx + 16)
                ps = pA[:, 512:1024] if ot % 2 == 0 else pA[:, 1024:1536]
                xf = ff[:, 0:512] if ot % 2 == 0 else ff[:, 512:1024]
                of = ff[:, 1024:1536] if ot % 2 == 0 else ff[:, 1536:2048]
                v.tensor_tensor(of, xf, ps, op=ALU.add).then_inc(
                    c.i("dv"), 1)

            @c.sec("scalar")
            def _(s, row0=row0, tt=tt, ot=ot):
                s.wait_ge(c.s("dv"), dvd + 1)
                of = ff[:, 1024:1536] if ot % 2 == 0 else ff[:, 1536:2048]
                s.dma_start(out=bass.AP(xov.tensor, row0 * T + tt * 512,
                                        [[8 * T, 128], [1, 1], [1, 512]]),
                            in_=of).then_inc(c.i("ls", 16), 16)


def emit_ffn_l1(c, wbT_d, bbT_d, wrT_d, acT_d, bcT_d, x_in, x_out):
    pA, pB = c.b["pA"], c.b["pB"]
    lnb = c.b["lnb"]
    BIG0, BIG1 = c.b["BIG0"], c.b["BIG1"]
    WA, WB = c.b["WA"], c.b["WB"]
    r8 = c.b["r8"]
    b0f = BIG0.bitcast(F32)
    b1f = BIG1.bitcast(F32)
    h_all = BIG0[:, 0:2048]
    g_all = BIG0[:, 2048:4096]
    lg_tm = b0f[:, 2048:2176]
    w_tm = b0f[:, 2176:2304]
    scol = [b0f[:, 2304 + i:2305 + i] for i in range(6)]
    e_scr = b0f[:, 2312:2320]
    w_fm = BIG0[0:8, 4736:6784]
    wb2 = BIG0[:, 8192:12288]
    xs = [b1f[:, 0:T], b1f[:, 0:T]]
    outf = [b1f[:, T:2 * T], b1f[:, 2 * T:3 * T]]
    IND, BB, identf = c.b["IND"], c.b["BB"], c.b["identf"]
    xiv = x_in.rearrange("(cc p) t -> cc p t", p=128)
    xov = x_out.rearrange("(cc p) t -> cc p t", p=128)
    dv0 = c.n["dv"]
    lg0 = c.n["lg"]

    @c.sec("gpsimd")
    def _(g):
        g.wait_ge(c.s("dv"), dv0)
        g.memset(identf[:, :], 1.0).then_inc(c.i("gp"), 1)
        g.wait_ge(c.s("gp"), c.n["gp"])
        g.affine_select(identf[:, :], identf[:, :], pattern=[[-1, 128]],
                        compare_op=ALU.is_equal, fill=0.0, base=0,
                        channel_multiplier=1).then_inc(c.i("gp"), 1)
        g.dma_start(out=WB[:, 0:4096],
                    in_=wslice_ap(acT_d, 128, 0, 128, D)
                    ).then_inc(c.i("lg", 16), 16)
        g.dma_start(out=WA[:, 0:4096],
                    in_=wslice_ap(bcT_d, 4096, 0, 4096, 128)
                    ).then_inc(c.i("lg", 16), 16)
        g.dma_start(out=wb2[:, 0:256],
                    in_=wslice_ap(wrT_d, 8, 0, 8, D)).then_inc(c.i("lg", 16), 16)

    @c.sec("sync")
    def _(q):
        q.dma_start(out=BB[:, :],
                    in_=bass.AP(bbT_d.tensor, 0,
                                [[32, 128], [1, 1], [1, 32]])
                    ).then_inc(c.i("ld", 16), 16)
    ld_bb = c.n["ld"]

    pe0 = c.n["pe"]

    @c.sec("tensor")
    def _(t):
        t.wait_ge(c.s("lg"), lg0 + 16)
        t.wait_ge(c.s("dv"), dv0)
        last = None
        for ch in range(DC):
            last = mm_tok(t, pA, WB[:, ch * 128:(ch + 1) * 128], lnb,
                          ch == 0, ch == DC - 1, coff=ch * T)
        last.then_inc(c.i("pe"), 1)

    @c.sec("vector")
    def _(v):
        v.wait_ge(c.s("pe"), pe0 + 1)
        v.tensor_copy(h_all, pA[:, :]).then_inc(c.i("dv"), 1)

    dvh = c.n["dv"]

    @c.sec("tensor")
    def _(t):
        t.wait_ge(c.s("lg"), lg0 + 48)
        t.wait_ge(c.s("dv"), dvh)
        last = None
        for ch in range(DC):
            last = mm_tok(t, pA[0:8, :], wb2[:, ch * 8:(ch + 1) * 8], lnb,
                          ch == 0, ch == DC - 1, coff=ch * T)
        last.then_inc(c.i("pe"), 1)

    pel = c.n["pe"]

    @c.sec("vector")
    def _(v):
        v.wait_ge(c.s("pe"), pel)
        v.tensor_copy(r8, pA[0:8, :]).then_inc(c.i("dv"), 1)

    dvl = c.n["dv"]

    gp_id = c.n["gp"]

    @c.sec("tensor")
    def _(t):
        t.wait_ge(c.s("dv"), dvl)
        t.wait_ge(c.s("gp"), gp_id)
        last = None
        for tt2 in range(16):
            last = t.transpose(pB[:, tt2 * 8:(tt2 + 1) * 8],
                               r8[0:8, tt2 * 128:(tt2 + 1) * 128],
                               identf[0:8, 0:8])
        last.then_inc(c.i("pe"), 1)

    pet = c.n["pe"]

    @c.sec("vector")
    def _(v):
        v.wait_ge(c.s("pe"), pet)
        v.tensor_copy(lg_tm, pB[:, 0:128]).then_inc(c.i("dv"), 1)

    dvt = c.n["dv"]
    act = c.n["ac"]

    for tt2 in range(16):
        lg_s = lg_tm[:, tt2 * 8:(tt2 + 1) * 8]
        w_s = w_tm[:, tt2 * 8:(tt2 + 1) * 8]

        @c.sec("vector")
        def _(v, lg_s=lg_s, w_s=w_s, tt2=tt2):
            if tt2 == 0:
                v.wait_ge(c.s("dv"), dvt)
            v.reduce_max(scol[0], lg_s, axis=AX.X).then_inc(c.i("dv"), 1)
            v.wait_ge(c.s("dv"), c.n["dv"])
            v.tensor_scalar(scol[1], scol[0], -1.0, None, op0=ALU.mult)
            v.tensor_scalar(w_s, lg_s, scol[0][:, 0:1], None,
                            op0=ALU.is_ge).then_inc(c.i("dv"), 1)
            v.wait_ge(c.s("dv"), c.n["dv"])
            v.scalar_tensor_tensor(w_s, w_s, -1e30, lg_s,
                                   op0=ALU.mult,
                                   op1=ALU.add).then_inc(c.i("dv"), 1)
            v.wait_ge(c.s("dv"), c.n["dv"])
            v.reduce_max(scol[2], w_s, axis=AX.X).then_inc(c.i("dv"), 1)

        @c.sec("scalar")
        def _(s, lg_s=lg_s, tt2=tt2):
            s.wait_ge(c.s("dv"), dvt + 7 * tt2 + 4)
            s.activation(e_scr, lg_s, AF.Exp, bias=scol[1][:, 0:1],
                         scale=1.0)
            s.activation(scol[4], scol[2], AF.Exp, bias=scol[1][:, 0:1],
                         scale=1.0).then_inc(c.i("ac"), 1)
            if tt2 == 0 and DBG_T2:
                s.copy(b0f[:, 2320:2326], b0f[:, 2304:2310])
                s.copy(b0f[:, 2326:2334], e_scr).then_inc(c.i("ac"), 1)

        @c.sec("vector")
        def _(v, lg_s=lg_s, w_s=w_s, tt2=tt2):
            v.wait_ge(c.s("ac"), act + tt2 + 1 + (1 if DBG_T2 else 0))
            v.tensor_scalar(w_s, lg_s, scol[2][:, 0:1], None,
                            op0=ALU.is_ge)
            v.tensor_scalar(scol[4], scol[4], 1.0, None,
                            op0=ALU.add).then_inc(c.i("dv"), 1)
            v.wait_ge(c.s("dv"), c.n["dv"])
            v.reciprocal(scol[5], scol[4]).then_inc(c.i("dv"), 1)
            v.wait_ge(c.s("dv"), c.n["dv"])
            v.tensor_tensor(w_s, w_s, e_scr, op=ALU.mult)
            v.tensor_scalar(w_s, w_s, scol[5][:, 0:1], None,
                            op0=ALU.mult).then_inc(c.i("dv"), 1)

    dvw = c.n["dv"]

    @c.sec("tensor")
    def _(t):
        t.wait_ge(c.s("dv"), dvw)
        last = None
        for tt2 in range(16):
            last = t.transpose(pB[0:8, tt2 * 128:(tt2 + 1) * 128],
                               w_tm[:, tt2 * 8:(tt2 + 1) * 8],
                               identf[0:128, 0:128])
        last.then_inc(c.i("pe"), 1)

    pew = c.n["pe"]

    @c.sec("vector")
    def _(v):
        v.wait_ge(c.s("pe"), pew)
        v.tensor_copy(w_fm[0:8, :], pB[0:8, 0:2048]).then_inc(c.i("dv"), 1)

    dvf = c.n["dv"]

    @c.sec("tensor")
    def _(t):
        t.wait_ge(c.s("dv"), dvf)
        mm_tok(t, pA, IND[0:8, :], w_fm[0:8, :], True,
               True).then_inc(c.i("pe"), 1)

    peb = c.n["pe"]

    @c.sec("vector")
    def _(v):
        v.wait_ge(c.s("pe"), peb)
        v.tensor_tensor(g_all, h_all, pA[:, :],
                        op=ALU.mult).then_inc(c.i("dv"), 1)

    dvg = c.n["dv"]
    st_base = {}

    for dm in range(DC):
        wdst = WB if dm % 2 == 0 else wb2
        pdst = pA if dm % 2 == 0 else pB

        @c.sec("gpsimd")
        def _(g, wdst=wdst, dm=dm):
            if dm == 0:
                g.wait_ge(c.s("dv"), dvg)
            if dm >= 2:
                g.wait_ge(c.s("pe"), c.n["pe"] - 1)
            g.dma_start(out=wdst[:, 0:4096],
                        in_=wslice_ap(wbT_d, D, dm * 128, 128, D)
                        ).then_inc(c.i("lg", 16), 16)

        ldx = c.n["ld"]

        @c.sec("sync")
        def _(q, dm=dm):
            if dm == 0:
                q.wait_ge(c.s("dv"), dvg)
            if dm >= 1:
                q.wait_ge(c.s("dv"), dvg + dm)
            q.dma_start(out=xs[dm % 2][:, :],
                        in_=xiv[dm, :, :]).then_inc(c.i("ld", 16), 16)

        lgw, pebl = c.n["lg"], c.n["pe"]

        @c.sec("tensor")
        def _(t, wdst=wdst, pdst=pdst, dm=dm):
            t.wait_ge(c.s("lg"), lgw)
            t.wait_ge(c.s("dv"), dvg + max(0, dm - 1))
            for ch in range(DC):
                mm_tok(t, pdst, wdst[:, ch * 128:(ch + 1) * 128], lnb,
                       ch == 0, False, coff=ch * T)
            mm_tok(t, pdst, WA[:, dm * 128:(dm + 1) * 128], g_all,
                   False, True).then_inc(c.i("pe"), 1)

        @c.sec("vector")
        def _(v, pdst=pdst, dm=dm):
            v.wait_ge(c.s("pe"), pebl + 1)
            v.wait_ge(c.s("ld"), ldx + 16)
            if dm == 0:
                v.wait_ge(c.s("ld"), ld_bb)
            if dm >= 2:
                v.wait_ge(c.s("ls"), st_base[dm - 2] + 16)
            v.scalar_tensor_tensor(outf[dm % 2], pdst[:, :],
                                   BB[:, dm:dm + 1], xs[dm % 2][:, :],
                                   op0=ALU.add,
                                   op1=ALU.add).then_inc(c.i("dv"), 1)

        dvo = c.n["dv"]
        st_base[dm] = c.n["ls"]

        @c.sec("scalar")
        def _(s, dm=dm):
            s.wait_ge(c.s("dv"), dvo)
            s.dma_start(out=xov[dm, :, :],
                        in_=outf[dm % 2]).then_inc(c.i("ls", 16), 16)


def build_l0a():
    c = Ctx()
    nc = c.nc
    x = nc.dram_tensor("x", [D, T], F32, kind="ExternalInput").ap()
    wq = nc.dram_tensor("wq", [D, D], F32, kind="ExternalInput").ap()
    wc = nc.dram_tensor("wc", [D, LAT], F32, kind="ExternalInput").ap()
    wk = nc.dram_tensor("wk", [LAT, D], F32, kind="ExternalInput").ap()
    wv = nc.dram_tensor("wv", [LAT, D], F32, kind="ExternalInput").ap()
    wo = nc.dram_tensor("wo", [D, D], F32, kind="ExternalInput").ap()
    x1 = nc.dram_tensor("x1", [D, T], F32, kind="ExternalOutput").ap()
    at = nc.dram_tensor("at", [D, T], BF16, kind="Internal").ap()
    c.start()
    alloc_common(c, "attn0")
    emit_rope_tables(c)
    emit_ln(c, x)
    emit_attn_l0(c, wq, wc, wk, wv, at)
    emit_wo_residual(c, wo, at, x, x1)
    c.fin()
    return c


def build_l0f():
    c = Ctx()
    nc = c.nc
    x1 = nc.dram_tensor("x1", [D, T], F32, kind="ExternalInput").ap()
    qwgu = nc.dram_tensor("qwgu", [D, 2752], I32, kind="ExternalInput").ap()
    qzgu = nc.dram_tensor("qzgu", [32, 2752], I32, kind="ExternalInput").ap()
    sgu = nc.dram_tensor("sgu", [32, 2 * INTER], F32,
                         kind="ExternalInput").ap()
    qwdn = nc.dram_tensor("qwdn", [INTER, 512], I32,
                          kind="ExternalInput").ap()
    qzdn = nc.dram_tensor("qzdn", [86, 512], I32, kind="ExternalInput").ap()
    sdn = nc.dram_tensor("sdn", [86, D], F32, kind="ExternalInput").ap()
    x2 = nc.dram_tensor("x2", [D, T], F32, kind="ExternalOutput").ap()
    h_scr = nc.dram_tensor("h_scr", [INTER, T], BF16, kind="Internal").ap()
    wdn4 = nc.dram_tensor("wdn4", [4096, 11264], BF16, kind="Internal").ap()
    c.start()
    alloc_common(c, "ffn0")
    emit_ln(c, x1)
    emit_gs(c)
    emit_gateup(c, qwgu, qzgu, sgu, h_scr)
    emit_wdn_dequant(c, qwdn, sdn, wdn4)
    emit_down(c, x1, x2, h_scr, wdn4, qzdn, sdn)
    c.fin()
    return c


def build_l1a():
    c = Ctx()
    nc = c.nc
    x2 = nc.dram_tensor("x2", [D, T], F32, kind="ExternalInput").ap()
    wq = nc.dram_tensor("wq", [D, D], F32, kind="ExternalInput").ap()
    wk = nc.dram_tensor("wk", [D, HD], F32, kind="ExternalInput").ap()
    wv = nc.dram_tensor("wv", [D, HD], F32, kind="ExternalInput").ap()
    wo = nc.dram_tensor("wo", [D, D], F32, kind="ExternalInput").ap()
    x3 = nc.dram_tensor("x3", [D, T], F32, kind="ExternalOutput").ap()
    at = nc.dram_tensor("at", [D, T], BF16, kind="Internal").ap()
    c.start()
    alloc_common(c, "attn1")
    emit_ln(c, x2)
    emit_attn_l1(c, wq, wk, wv, at)
    emit_wo_residual(c, wo, at, x2, x3)
    c.fin()
    return c


def build_l1f():
    c = Ctx()
    nc = c.nc
    x3 = nc.dram_tensor("x3", [D, T], F32, kind="ExternalInput").ap()
    wbT = nc.dram_tensor("wbT", [D, D], F32, kind="ExternalInput").ap()
    bbT = nc.dram_tensor("bbT", [128, 32], F32, kind="ExternalInput").ap()
    wrT = nc.dram_tensor("wrT", [D, 8], F32, kind="ExternalInput").ap()
    acT = nc.dram_tensor("acT", [D, 128], F32, kind="ExternalInput").ap()
    bcT = nc.dram_tensor("bcT", [128, D], F32, kind="ExternalInput").ap()
    xo = nc.dram_tensor("xo", [D, T], F32, kind="ExternalOutput").ap()
    c.start()
    alloc_common(c, "ffn1")
    emit_ln(c, x3)
    emit_ffn_l1(c, wbT, bbT, wrT, acT, bcT, x3, xo)
    c.fin()
    return c


def _prep_weights(inputs):
    w = {}
    w["l0_wq"] = np.ascontiguousarray(inputs["L0_attn_Wq"])
    w["l0_wc"] = np.ascontiguousarray(inputs["L0_attn_Wc"])
    w["l0_wk"] = np.ascontiguousarray(inputs["L0_attn_Wk"])
    w["l0_wv"] = np.ascontiguousarray(inputs["L0_attn_Wv"])
    w["l0_wo"] = np.ascontiguousarray(inputs["L0_attn_Wo"])
    w["qwgu"] = np.ascontiguousarray(inputs["L0_ffn_qweight_gu"])
    w["qzgu"] = np.ascontiguousarray(inputs["L0_ffn_qzeros_gu"])
    s = inputs["L0_ffn_scales_gu"]
    w["sgu"] = np.ascontiguousarray(
        np.concatenate([s[:, NHAT], s[:, INTER + NHAT]], axis=1))
    w["qwdn"] = np.ascontiguousarray(inputs["L0_ffn_qweight_dn"][NHAT, :])
    w["qzdn"] = np.ascontiguousarray(inputs["L0_ffn_qzeros_dn"])
    w["sdn"] = np.ascontiguousarray(inputs["L0_ffn_scales_dn"][:, KHAT])
    w["l1_wq"] = np.ascontiguousarray(inputs["L1_attn_wq"])
    w["l1_wk"] = np.ascontiguousarray(inputs["L1_attn_wk"])
    w["l1_wv"] = np.ascontiguousarray(inputs["L1_attn_wv"])
    w["l1_wo"] = np.ascontiguousarray(inputs["L1_attn_wo"])
    w["wbT"] = np.ascontiguousarray(inputs["L1_ffn_W_base"].T)
    w["bbT"] = np.ascontiguousarray(
        inputs["L1_ffn_b_base"].reshape(32, 128).T)
    w["wrT"] = np.ascontiguousarray(inputs["L1_ffn_W_router"].T)
    w["acT"] = np.ascontiguousarray(
        inputs["L1_ffn_A"].reshape(128, D).T)
    w["bcT"] = np.ascontiguousarray(
        inputs["L1_ffn_B"].transpose(0, 2, 1).reshape(128, D))
    return w


def kernel(**inputs):
    return kernel_fused(inputs)


def kernel_multilaunch(**inputs):
    NCORES = 8
    x = np.asarray(inputs["L0_attn_hidden_states"])
    w = _prep_weights(inputs)
    xT = [np.ascontiguousarray(x[i].T) for i in range(NCORES)]

    c = build_l0a()
    maps = [dict(x=xT[i], wq=w["l0_wq"], wc=w["l0_wc"], wk=w["l0_wk"],
                 wv=w["l0_wv"], wo=w["l0_wo"]) for i in range(NCORES)]
    r = run_bass_kernel_spmd(c.nc, maps, core_ids=list(range(NCORES)))
    c.close()
    x1 = [r.results[i]["x1"] for i in range(NCORES)]

    c = build_l0f()
    maps = [dict(x1=x1[i], qwgu=w["qwgu"], qzgu=w["qzgu"], sgu=w["sgu"],
                 qwdn=w["qwdn"], qzdn=w["qzdn"], sdn=w["sdn"])
            for i in range(NCORES)]
    r = run_bass_kernel_spmd(c.nc, maps, core_ids=list(range(NCORES)))
    c.close()
    x2 = [r.results[i]["x2"] for i in range(NCORES)]

    c = build_l1a()
    maps = [dict(x2=x2[i], wq=w["l1_wq"], wk=w["l1_wk"], wv=w["l1_wv"],
                 wo=w["l1_wo"]) for i in range(NCORES)]
    r = run_bass_kernel_spmd(c.nc, maps, core_ids=list(range(NCORES)))
    c.close()
    x3 = [r.results[i]["x3"] for i in range(NCORES)]

    c = build_l1f()
    maps = [dict(x3=x3[i], wbT=w["wbT"], bbT=w["bbT"], wrT=w["wrT"],
                 acT=w["acT"], bcT=w["bcT"]) for i in range(NCORES)]
    r = run_bass_kernel_spmd(c.nc, maps, core_ids=list(range(NCORES)))
    c.close()
    out = np.stack([r.results[i]["xo"].T for i in range(NCORES)], axis=0)
    return np.ascontiguousarray(out.astype(np.float32))


def build_fused():
    c = Ctx()
    nc = c.nc
    x = nc.dram_tensor("x", [D, T], F32, kind="ExternalInput").ap()
    wq0 = nc.dram_tensor("wq0", [D, D], F32, kind="ExternalInput").ap()
    wc0 = nc.dram_tensor("wc0", [D, LAT], F32, kind="ExternalInput").ap()
    wk0 = nc.dram_tensor("wk0", [LAT, D], F32, kind="ExternalInput").ap()
    wv0 = nc.dram_tensor("wv0", [LAT, D], F32, kind="ExternalInput").ap()
    wo0 = nc.dram_tensor("wo0", [D, D], F32, kind="ExternalInput").ap()
    qwgu = nc.dram_tensor("qwgu", [D, 2752], I32, kind="ExternalInput").ap()
    qzgu = nc.dram_tensor("qzgu", [32, 2752], I32, kind="ExternalInput").ap()
    sgu = nc.dram_tensor("sgu", [32, 2 * INTER], F32,
                         kind="ExternalInput").ap()
    qwdn = nc.dram_tensor("qwdn", [INTER, 512], I32,
                          kind="ExternalInput").ap()
    qzdn = nc.dram_tensor("qzdn", [86, 512], I32, kind="ExternalInput").ap()
    sdn = nc.dram_tensor("sdn", [86, D], F32, kind="ExternalInput").ap()
    wq1 = nc.dram_tensor("wq1", [D, D], F32, kind="ExternalInput").ap()
    wk1 = nc.dram_tensor("wk1", [D, HD], F32, kind="ExternalInput").ap()
    wv1 = nc.dram_tensor("wv1", [D, HD], F32, kind="ExternalInput").ap()
    wo1 = nc.dram_tensor("wo1", [D, D], F32, kind="ExternalInput").ap()
    wbT = nc.dram_tensor("wbT", [D, D], F32, kind="ExternalInput").ap()
    bbT = nc.dram_tensor("bbT", [128, 32], F32, kind="ExternalInput").ap()
    wrT = nc.dram_tensor("wrT", [D, 8], F32, kind="ExternalInput").ap()
    acT = nc.dram_tensor("acT", [D, 128], F32, kind="ExternalInput").ap()
    bcT = nc.dram_tensor("bcT", [128, D], F32, kind="ExternalInput").ap()
    xo = nc.dram_tensor("xo", [D, T], F32, kind="ExternalOutput").ap()
    x1 = nc.dram_tensor("x1", [D, T], F32, kind="Internal").ap()
    x2 = nc.dram_tensor("x2", [D, T], F32, kind="Internal").ap()
    x3 = nc.dram_tensor("x3", [D, T], F32, kind="Internal").ap()
    at = nc.dram_tensor("at", [D, T], BF16, kind="Internal").ap()
    h_scr = nc.dram_tensor("h_scr", [INTER, T], BF16, kind="Internal").ap()
    wdn4 = nc.dram_tensor("wdn4", [4096, 11264], BF16, kind="Internal").ap()

    c.start()
    alloc_common(c, "fused")
    emit_rope_tables(c)
    emit_ln(c, x)
    c.barrier()
    emit_attn_l0(c, wq0, wc0, wk0, wv0, at)
    c.barrier()
    emit_wo_residual(c, wo0, at, x, x1)
    c.barrier()
    emit_ln(c, x1)
    c.barrier()
    emit_gs(c)
    c.barrier()
    emit_gateup(c, qwgu, qzgu, sgu, h_scr)
    emit_wdn_dequant(c, qwdn, sdn, wdn4)
    emit_down(c, x1, x2, h_scr, wdn4, qzdn, sdn)
    c.barrier()
    emit_ln(c, x2)
    c.barrier()
    emit_attn_l1(c, wq1, wk1, wv1, at)
    c.barrier()
    emit_wo_residual(c, wo1, at, x2, x3)
    c.barrier()
    emit_ln(c, x3)
    c.barrier()
    emit_ffn_l1(c, wbT, bbT, wrT, acT, bcT, x3, xo)
    c.fin()
    return c


def kernel_fused(inputs):
    NCORES = 8
    x = np.asarray(inputs["L0_attn_hidden_states"])
    w = _prep_weights(inputs)
    xT = [np.ascontiguousarray(x[i].T) for i in range(NCORES)]
    c = build_fused()
    maps = [dict(x=xT[i], wq0=w["l0_wq"], wc0=w["l0_wc"], wk0=w["l0_wk"],
                 wv0=w["l0_wv"], wo0=w["l0_wo"], qwgu=w["qwgu"],
                 qzgu=w["qzgu"], sgu=w["sgu"], qwdn=w["qwdn"],
                 qzdn=w["qzdn"], sdn=w["sdn"], wq1=w["l1_wq"],
                 wk1=w["l1_wk"], wv1=w["l1_wv"], wo1=w["l1_wo"],
                 wbT=w["wbT"], bbT=w["bbT"], wrT=w["wrT"], acT=w["acT"],
                 bcT=w["bcT"]) for i in range(NCORES)]
    r = run_bass_kernel_spmd(c.nc, maps, core_ids=list(range(NCORES)))
    c.close()
    out = np.stack([r.results[i]["xo"].T for i in range(NCORES)], axis=0)
    return np.ascontiguousarray(out.astype(np.float32))
